# version 23
# baseline (speedup 1.0000x reference)
import numpy as np
import ml_dtypes
from contextlib import ExitStack
import concourse.bass as bass
import concourse.mybir as mybir
from concourse.bass_utils import run_bass_kernel_spmd

F32 = mybir.dt.float32
BF16 = mybir.dt.bfloat16
I32 = mybir.dt.int32
AF = mybir.ActivationFunctionType
ALU = mybir.AluOpType
AX = mybir.AxisListType

D = 1024
S = 4096
MY = 2048
CH = 512
NCH = S // CH
NMYCH = MY // CH
INV_D = 1.0 / D
EPS = 1e-5


class Sem:
    __slots__ = ("h", "n")
    def __init__(self, h):
        self.h = h
        self.n = 0


class Plan:
    ENGS = ("te", "ve", "se", "ge", "sy")

    def __init__(self):
        self.q = {e: [] for e in self.ENGS}

    def add(self, eng, emit, wait=(), inc=()):
        best = {}
        for s, v in wait:
            if v > best.get(id(s), (None, -1))[1]:
                best[id(s)] = (s, v)
        self.q[eng].append((tuple(best.values()), emit, tuple(inc)))
        for s, k in inc:
            s.n += k

    def replay_into(self, nc):
        qs = self.q
        with nc.Block() as blk:
            if qs["te"]:
                blk.tensor(lambda h: Plan._run(h, qs["te"]))
            if qs["ve"]:
                blk.vector(lambda h: Plan._run(h, qs["ve"]))
            if qs["se"]:
                blk.scalar(lambda h: Plan._run(h, qs["se"]))
            if qs["ge"]:
                blk.gpsimd(lambda h: Plan._run(h, qs["ge"]))
            if qs["sy"]:
                blk.sync(lambda h: Plan._run(h, qs["sy"]))
        self.q = {e: [] for e in self.ENGS}

    @staticmethod
    def _run(h, q):
        for wait, emit, inc in q:
            for s, v in wait:
                if v > 0:
                    h.wait_ge(s.h, v)
            inst = emit(h)
            for s, k in inc:
                inst.then_inc(s.h, k)


def build(debug=False):
    nc = bass.Bass("TRN2", target_bir_lowering=False)

    xf = nc.dram_tensor("xf", [S, D], F32, kind="ExternalInput")
    xm = nc.dram_tensor("xm", [MY, D], F32, kind="ExternalInput")
    wq = nc.dram_tensor("wq", [D, 512], F32, kind="ExternalInput")
    wk = nc.dram_tensor("wk", [D, 512], F32, kind="ExternalInput")
    wv = nc.dram_tensor("wv", [D, 512], F32, kind="ExternalInput")
    wo = nc.dram_tensor("wo", [D, D], F32, kind="ExternalInput")
    bqh = nc.dram_tensor("bqh", [128, 4], F32, kind="ExternalInput")
    bkh = nc.dram_tensor("bkh", [128, 4], F32, kind="ExternalInput")
    bvh = nc.dram_tensor("bvh", [1, 512], F32, kind="ExternalInput")
    boh = nc.dram_tensor("boh", [1, D], F32, kind="ExternalInput")
    winh = nc.dram_tensor("winh", [D, 64], F32, kind="ExternalInput")
    binh = nc.dram_tensor("binh", [64, 1], F32, kind="ExternalInput")
    w1h = nc.dram_tensor("w1h", [64, 16], F32, kind="ExternalInput")
    b1h = nc.dram_tensor("b1h", [64, 16], F32, kind="ExternalInput")
    w2h = nc.dram_tensor("w2h", [64, 16], F32, kind="ExternalInput")
    b2h = nc.dram_tensor("b2h", [64, 1], F32, kind="ExternalInput")
    wouth = nc.dram_tensor("wouth", [64, D], F32, kind="ExternalInput")
    bouth = nc.dram_tensor("bouth", [1, D], F32, kind="ExternalInput")
    qoff = nc.dram_tensor("qoff", [1, 1], I32, kind="ExternalInput")
    out = nc.dram_tensor("out", [MY, D], F32, kind="ExternalOutput")
    ag_in2 = nc.dram_tensor("ag_in2", [1024, MY], BF16)
    ag_outs = [nc.dram_tensor(f"ag_out{j}", [512, MY], BF16) for j in range(4)]

    dbg = {}
    if debug:
        dbg["qT"] = nc.dram_tensor("d_qT", [128, 4 * S], BF16, kind="ExternalOutput")
        dbg["kT"] = nc.dram_tensor("d_kT", [128, 4 * S], BF16, kind="ExternalOutput")
        dbg["vv"] = nc.dram_tensor("d_vv", [128, 8 * 32 * 65], BF16, kind="ExternalOutput")
        dbg["s"] = nc.dram_tensor("d_s", [64, MY], F32, kind="ExternalOutput")
        dbg["hT"] = nc.dram_tensor("d_hT", [128, 4 * S], BF16, kind="ExternalOutput")
        dbg["gT"] = nc.dram_tensor("d_gT", [64, MY], BF16, kind="ExternalOutput")

    with ExitStack() as stP:
        EP = stP.enter_context
        stAttn = ExitStack()
        qT = stAttn.enter_context(nc.sbuf_tensor("qT", [128, 4 * S], BF16, side="right"))
        kT = stAttn.enter_context(nc.sbuf_tensor("kT", [128, 4 * S], BF16, side="right"))
        vv = stAttn.enter_context(nc.sbuf_tensor("vv", [128, 8 * 32 * 65], BF16, side="right"))
        s_sb = stAttn.enter_context(nc.sbuf_tensor("s_sb", [64, MY], F32, side="right"))
        gT = EP(nc.sbuf_tensor("gT", [64, MY], BF16))
        ident = EP(nc.sbuf_tensor("ident", [128, 128], BF16))
        ones1 = EP(nc.sbuf_tensor("ones1", [1, 128], BF16))
        bq_sb = EP(nc.sbuf_tensor("bq_sb", [128, 4], F32))
        bk_sb = EP(nc.sbuf_tensor("bk_sb", [128, 4], F32))
        bv_sb = EP(nc.sbuf_tensor("bv_sb", [1, 512], BF16))
        bo_sb = EP(nc.sbuf_tensor("bo_sb", [1, D], BF16))
        bout_sb = EP(nc.sbuf_tensor("bout_sb", [1, D], BF16))
        bin_sb = EP(nc.sbuf_tensor("bin_sb", [64, 1], F32))
        w1_sb = EP(nc.sbuf_tensor("w1_sb", [64, 16], F32))
        b1_sb = EP(nc.sbuf_tensor("b1_sb", [64, 16], F32))
        w2_sb = EP(nc.sbuf_tensor("w2_sb", [64, 16], F32))
        b2_sb = EP(nc.sbuf_tensor("b2_sb", [64, 1], F32))
        qoff_sb = EP(nc.sbuf_tensor("qoff_sb", [1, 1], I32))
        eps_sb = EP(nc.sbuf_tensor("eps_sb", [128, 1], F32))

        te = Sem(EP(nc.semaphore("te_s")))
        ve = Sem(EP(nc.semaphore("ve_s")))
        se = Sem(EP(nc.semaphore("se_s")))
        ge = Sem(EP(nc.semaphore("ge_s")))
        cs = Sem(EP(nc.semaphore("cs_s")))
        xsb = [Sem(EP(nc.semaphore(f"xsb{i}_s"))) for i in range(4)]
        wsg = {k: Sem(EP(nc.semaphore(f"ws_{k}_s")))
               for k in ("wc", "wq", "wk", "wv", "win", "wo", "wout")}
        gs = Sem(EP(nc.semaphore("gs_s")))
        agst = [Sem(EP(nc.semaphore(f"agst{i}_s"))) for i in range(8)]
        ccsj = [Sem(EP(nc.semaphore(f"ccsj{i}_s"))) for i in range(4)]
        osb2 = [Sem(EP(nc.semaphore(f"osb2{i}_s"))) for i in range(2)]
        os_ = Sem(EP(nc.semaphore("os_s")))

        P = Plan()

        for src, dst in ((bqh, bq_sb), (bkh, bk_sb), (binh, bin_sb),
                         (w1h, w1_sb), (b1h, b1_sb), (w2h, w2_sb), (b2h, b2_sb),
                         (qoff, qoff_sb)):
            P.add("sy", lambda h, a=dst, b=src: h.dma_start(a[:, :], b.ap()),
                  inc=[(cs, 16)])
        for src, dst in ((bvh, bv_sb), (boh, bo_sb), (bouth, bout_sb)):
            P.add("ge", lambda h, a=dst, b=src: h.dma_start(a[:, :], b.ap()),
                  inc=[(wsg["wc"], 16)])
        m_bv = wsg["wc"].n
        P.replay_into(nc)

        with ExitStack() as stA:
            EA = stA.enter_context
            wq_bf = EA(nc.sbuf_tensor("wq_bf", [128, 8 * 512], BF16, side="right"))
            wk_bf = EA(nc.sbuf_tensor("wk_bf", [128, 8 * 512], BF16, side="right"))
            wv_bf = EA(nc.sbuf_tensor("wv_bf", [128, 8 * 512], BF16, side="right"))
            win_bf = EA(nc.sbuf_tensor("win_bf", [128, 8 * 64], BF16, side="right"))
            x_t = [EA(nc.sbuf_tensor(f"x_t{i}", [128, D], F32, side="right")) for i in range(4)]
            xn_t = [EA(nc.sbuf_tensor(f"xn_t{i}", [128, D], BF16, side="right")) for i in range(4)]
            xnT = [EA(nc.sbuf_tensor(f"xnT{i}", [128, 8 * CH], BF16, side="right")) for i in range(2)]
            sx = EA(nc.sbuf_tensor("sx", [128, 8], F32, side="right"))
            ssq = EA(nc.sbuf_tensor("ssq", [128, 8], F32, side="right"))
            nm = EA(nc.sbuf_tensor("nm", [128, 8], F32, side="right"))
            var = EA(nc.sbuf_tensor("var", [128, 8], F32, side="right"))
            m2t = EA(nc.sbuf_tensor("m2t", [128, 8], F32, side="right"))
            tln = EA(nc.sbuf_tensor("tln", [128, 8], F32, side="right"))
            rstd = EA(nc.sbuf_tensor("rstd", [128, 8], F32, side="right"))

            with ExitStack() as stPs1:
                E1 = stPs1.enter_context
                ps_tr = [E1(nc.psum_tensor(f"ps_tr{i}", [128, 1024], BF16)) for i in range(2)]
                ps_p = [E1(nc.psum_tensor(f"ps_p{i}", [128, 512], F32)) for i in range(4)]

                P.add("ge", lambda h: h.iota(ident[:, :], [[1, 128]], base=0,
                                             channel_multiplier=-1,
                                             allow_small_or_imprecise_dtypes=True),
                      inc=[(ge, 1)])
                m_iota = ge.n
                P.add("ve", lambda h: h.tensor_single_scalar(
                    ident[:, :], ident[:, :], 0.0, ALU.is_equal),
                    wait=[(ge, m_iota)], inc=[(ve, 1)])
                P.add("ve", lambda h: h.memset(ones1[:, :], 1.0),
                      wait=[(ve, ve.n)], inc=[(ve, 1)])
                P.add("ve", lambda h: h.memset(
                    bass.AP(vv, 64, [[8 * 32 * 65, 128], [65, 8 * 32], [1, 1]]), 1.0),
                    wait=[(ve, ve.n)], inc=[(ve, 1)])
                P.add("ve", lambda h: h.memset(eps_sb[:, :], EPS),
                      wait=[(ve, ve.n)], inc=[(ve, 1)])
                m_const = ve.n

                wmarks = {}
                for name, wsrc, wdst, nblk, bw in (
                        ("wq", wq, wq_bf, 8, 512), ("wk", wk, wk_bf, 8, 512),
                        ("wv", wv, wv_bf, 8, 512), ("win", winh, win_bf, 8, 64)):
                    for db in range(nblk):
                        P.add("ge", lambda h, a=wdst, b=wsrc, db=db, bw=bw, name=name:
                              h.dma_start(a[:, db * bw:(db + 1) * bw],
                                          b[db * 128:(db + 1) * 128, :]),
                              inc=[(wsg[name], 16)])
                    wmarks[name] = (wsg[name], wsg[name].n)

                xt_free = [0, 0, 0, 0]
                xn_free = [0, 0, 0, 0]
                tr_free = [0, 0]
                bank_free = [(ve, 0)] * 4
                ch_done = {}
                bank_i = 0

                for c in range(NCH + NMYCH):
                    bc = c % 2
                    sc4 = (c % 2) * 4
                    full = c < NCH
                    for half in range(2):
                        pair = (2 * half, 2 * half + 1)
                        last_sum = 0
                        for ti in pair:
                            xb = ti
                            col = sc4 + ti
                            if full:
                                src = xf[c * CH + ti * 128: c * CH + (ti + 1) * 128, :]
                            else:
                                base = (c - NCH) * CH + ti * 128
                                src = xm[base: base + 128, :]
                            P.add("sy", lambda h, a=x_t[xb], b=src: h.dma_start(a[:, :], b),
                                  wait=[(ve, xt_free[xb])], inc=[(xsb[xb], 16)])
                            m_x = (xsb[xb], xsb[xb].n)
                            P.add("se", lambda h, a=x_t[xb], o=xn_t[ti], col=col:
                                h.activation(
                                    o[:, :], a[:, :], AF.Identity,
                                    accum_out=sx[:, col:col + 1]),
                                wait=[m_x, (se, se.n), (te, xn_free[ti])],
                                inc=[(se, 1)])
                            last_sum = se.n
                            P.add("se", lambda h, a=x_t[xb], o=xn_t[ti], col=col:
                                h.activation(
                                    o[:, :], a[:, :], AF.Square,
                                    accum_out=ssq[:, col:col + 1]),
                                wait=[m_x, (se, se.n), (te, xn_free[ti])],
                                inc=[(se, 1)])
                            last_sq = se.n
                        s2 = sc4 + 2 * half
                        P.add("ve", lambda h, s2=s2: h.tensor_scalar(
                            nm[:, s2:s2 + 2], sx[:, s2:s2 + 2], -INV_D, None, ALU.mult),
                            wait=[(se, last_sum), (ve, ve.n)], inc=[(ve, 1)])
                        P.add("ve", lambda h, s2=s2: h.tensor_mul(
                            m2t[:, s2:s2 + 2], nm[:, s2:s2 + 2], nm[:, s2:s2 + 2]),
                            wait=[(ve, ve.n)], inc=[(ve, 1)])
                        P.add("ve", lambda h, s2=s2: h.scalar_tensor_tensor(
                            var[:, s2:s2 + 2], ssq[:, s2:s2 + 2], INV_D, m2t[:, s2:s2 + 2],
                            ALU.mult, ALU.subtract),
                            wait=[(ve, ve.n), (se, last_sq)], inc=[(ve, 1)])
                        m_var = ve.n
                        P.add("se", lambda h, s2=s2: h.activation(
                            tln[:, s2:s2 + 2], var[:, s2:s2 + 2], AF.Ln,
                            bias=eps_sb[:, 0:1]),
                            wait=[(ve, m_var), (se, se.n), (ve, m_const)], inc=[(se, 1)])
                        P.add("se", lambda h, s2=s2: h.activation(
                            rstd[:, s2:s2 + 2], tln[:, s2:s2 + 2], AF.Exp, scale=-0.5),
                            wait=[(se, se.n)], inc=[(se, 1)])
                        m_rstd = se.n
                        m_trs = {}
                        for ti in pair:
                            xb = ti
                            col = sc4 + ti
                            P.add("ve", lambda h, a=xn_t[ti], b=x_t[xb], col=col: h.tensor_scalar(
                                a[:, :], b[:, :], nm[:, col:col + 1], rstd[:, col:col + 1],
                                ALU.add, ALU.mult),
                                wait=[(se, m_rstd), (te, xn_free[ti]), (ve, ve.n)],
                                inc=[(ve, 1)])
                            xt_free[xb] = ve.n
                            m_xn = ve.n
                            tb = ti % 2
                            for db in range(8):
                                P.add("te", lambda h, a=ps_tr[tb], b=xn_t[ti], db=db: h.transpose(
                                    a[:, db * 128:(db + 1) * 128],
                                    b[:, db * 128:(db + 1) * 128], ident[:, :]),
                                    wait=([(ve, m_xn), (ve, tr_free[tb]), (ve, m_const)]
                                          if db == 0 else []),
                                    inc=([(te, 1)] if db == 7 else []))
                            xn_free[ti] = te.n
                            m_trs[ti] = te.n
                        for ti in pair:
                            tb = ti % 2
                            P.add("ve", lambda h, a=xnT[bc], b=ps_tr[tb], ti=ti: h.tensor_copy(
                                bass.AP(a, ti * 128, [[8 * CH, 128], [CH, 8], [1, 128]]),
                                b[:, :]),
                                wait=[(te, m_trs[ti]), (te, ch_done.get(c - 2, 0)), (ve, ve.n)],
                                inc=[(ve, 1)])
                            tr_free[tb] = ve.n
                    m_evac = ve.n
                    if full:
                        for wb, bias_t, dst, wmk in ((wq_bf, bq_sb, qT, wmarks["wq"]),
                                                     (wk_bf, bk_sb, kT, wmarks["wk"])):
                            for j in range(4):
                                bk = bank_i % 4
                                bank_i += 1
                                for db in range(8):
                                    P.add("te", lambda h, bk=bk, wb=wb, db=db, j=j, bc=bc:
                                          h.matmul(ps_p[bk][:, :],
                                                   wb[:, db * 512 + j * 128: db * 512 + (j + 1) * 128],
                                                   xnT[bc][:, db * CH:(db + 1) * CH],
                                                   start=(db == 0), stop=(db == 7)),
                                          wait=([(ve, m_evac), bank_free[bk], wmk]
                                                if db == 0 else []),
                                          inc=([(te, 1)] if db == 7 else []))
                                m_mm = te.n
                                P.add("se", lambda h, bk=bk, dst=dst, bias_t=bias_t, j=j, c=c:
                                      h.activation(dst[:, j * S + c * CH: j * S + (c + 1) * CH],
                                                   ps_p[bk][:, :], AF.Identity,
                                                   bias=bias_t[:, j:j + 1]),
                                      wait=[(se, se.n), (te, m_mm), (cs, 16 * 8)],
                                      inc=[(se, 1)])
                                bank_free[bk] = (se, se.n)
                        for ti in range(4):
                            bk = bank_i % 4
                            bank_i += 1
                            for db in range(8):
                                P.add("te", lambda h, bk=bk, db=db, ti=ti, bc=bc:
                                      h.matmul(ps_p[bk][:, :],
                                               xnT[bc][:, db * CH + ti * 128: db * CH + (ti + 1) * 128],
                                               wv_bf[:, db * 512:(db + 1) * 512],
                                               start=(db == 0), stop=False),
                                      wait=([(ve, m_evac), bank_free[bk],
                                             wmarks["wv"]] if db == 0 else []))
                            P.add("te", lambda h, bk=bk: h.matmul(
                                ps_p[bk][:, :], ones1[0:1, 0:128], bv_sb[0:1, :],
                                start=False, stop=True),
                                wait=[(wsg["wc"], m_bv)], inc=[(te, 1)])
                            m_mm = te.n
                            tb_idx = c * 4 + ti
                            P.add("ve", lambda h, bk=bk, tb_idx=tb_idx: h.tensor_copy(
                                bass.AP(vv, tb_idx * 65,
                                        [[8 * 32 * 65, 128], [32 * 65, 8], [1, 64]]),
                                ps_p[bk][:, :]),
                                wait=[(te, m_mm), (ve, ve.n)], inc=[(ve, 1)])
                            bank_free[bk] = (ve, ve.n)
                    else:
                        bk = bank_i % 4
                        bank_i += 1
                        for db in range(8):
                            P.add("te", lambda h, bk=bk, db=db, bc=bc: h.matmul(
                                ps_p[bk][0:64, :],
                                win_bf[:, db * 64:(db + 1) * 64],
                                xnT[bc][:, db * CH:(db + 1) * CH],
                                start=(db == 0), stop=(db == 7)),
                                wait=([(ve, m_evac), bank_free[bk], wmarks["win"]]
                                      if db == 0 else []),
                                inc=([(te, 1)] if db == 7 else []))
                        m_mm = te.n
                        cc = c - NCH
                        P.add("se", lambda h, bk=bk, cc=cc: h.activation(
                            s_sb[0:64, cc * CH:(cc + 1) * CH], ps_p[bk][0:64, :],
                            AF.Identity, bias=bin_sb[0:64, 0:1]),
                            wait=[(se, se.n), (te, m_mm)], inc=[(se, 1)])
                        bank_free[bk] = (se, se.n)
                    ch_done[c] = te.n

                P.replay_into(nc)

        stB2 = ExitStack()
        hT = stB2.enter_context(nc.sbuf_tensor("hT", [128, 4 * S], BF16))
        wo_bf = stB2.enter_context(nc.sbuf_tensor("wo_bf", [128, 8 * D], BF16))
        wout_bf = stB2.enter_context(nc.sbuf_tensor("wout_bf", [64, D], BF16))
        for db in range(8):
            P.add("ge", lambda h, db=db: h.dma_start(
                wo_bf[:, db * D:(db + 1) * D], wo[db * 128:(db + 1) * 128, :]),
                inc=[(wsg["wo"], 16)])
        wmarks["wo"] = (wsg["wo"], wsg["wo"].n)
        P.add("ge", lambda h: h.dma_start(wout_bf[:, :], wouth.ap()),
              inc=[(wsg["wout"], 16)])
        wmarks["wout"] = (wsg["wout"], wsg["wout"].n)
        with ExitStack() as stB:
            EB = stB.enter_context
            probs = [EB(nc.sbuf_tensor(f"probs{i}", [128, 1024], BF16, side="right")) for i in range(6)]
            hTun = EB(nc.sbuf_tensor("hTun", [64, 1024], BF16, side="right"))
            linv_f = EB(nc.sbuf_tensor("linv_f", [1, 1024], F32, side="right"))
            linv_b = EB(nc.sbuf_tensor("linv_b", [1, 1024], BF16, side="right"))
            h_t = EB(nc.sbuf_tensor("h_t", [64, MY], F32, side="right"))
            g_t = EB(nc.sbuf_tensor("g_t", [64, MY], F32, side="right"))
            maskt = EB(nc.sbuf_tensor("maskt", [128, 4 * 512], BF16, side="right"))
            with ExitStack() as stPs2:
                E2 = stPs2.enter_context
                sAB = [E2(nc.psum_tensor(f"sAB{i}", [128, 1024], F32)) for i in range(2)]
                hpsT = [E2(nc.psum_tensor(f"hpsT{i}", [65, 1024], F32)) for i in range(2)]

                P.add("ve", lambda h: h.memset(maskt[:, :], 1.0), wait=[(ve, ve.n)],
                      inc=[(ve, 1)])
                m_ones = ve.n
                for dlt in range(4):
                    P.add("ge", lambda h, dlt=dlt: h.affine_select(
                        maskt[:, dlt * 512:(dlt + 1) * 512],
                        maskt[:, dlt * 512:(dlt + 1) * 512],
                        [[1, 512]], ALU.is_ge, 0.0,
                        base=-dlt * 128, channel_multiplier=-1),
                        wait=[(ve, m_ones), (ge, ge.n)], inc=[(ge, 1)])
                m_maskt = ge.n

                st = {
                    "arena_free": [0, 0],
                    "probs_free": [0] * 6,
                    "hps_free": [0, 0],
                    "linv_rd": 0,
                    "hTun_rd": 0,
                    "gg": 0,
                    "ffn_g": 0,
                    "ffn_left": list(range(16)),
                }

                def emit_ffn_step():
                    if not st["ffn_left"]:
                        return
                    i = st["ffn_left"].pop(0)
                    P.add("ve", lambda h, i=i: h.tensor_scalar(
                        h_t[:, :], s_sb[:, :], w1_sb[:, i:i + 1], b1_sb[:, i:i + 1],
                        ALU.mult, ALU.add),
                        wait=[(ve, st["ffn_g"]), (ve, ve.n)], inc=[(ve, 1)])
                    P.add("ve", lambda h: h.tensor_scalar_max(h_t[:, :], h_t[:, :], 0.0),
                          wait=[(ve, ve.n)], inc=[(ve, 1)])
                    m_h = ve.n
                    if i == 0:
                        P.add("ve", lambda h: h.tensor_scalar(
                            g_t[:, :], h_t[:, :], w2_sb[:, 0:1], b2_sb[:, 0:1],
                            ALU.mult, ALU.add),
                            wait=[(ve, m_h)], inc=[(ve, 1)])
                    else:
                        P.add("ve", lambda h, i=i: h.scalar_tensor_tensor(
                            g_t[:, :], h_t[:, :], w2_sb[:, i:i + 1], g_t[:, :],
                            ALU.mult, ALU.add),
                            wait=[(ve, m_h)], inc=[(ve, 1)])
                    st["ffn_g"] = ve.n

                def emit_fin_a(jj, cc, par, m_av):
                    P.add("se", lambda h, par=par: h.activation(
                        linv_f[0:1, :], hpsT[par][64:65, :], AF.Ln),
                        wait=[(te, m_av), (se, se.n), (te, st["linv_rd"])],
                        inc=[(se, 1)])
                    P.add("se", lambda h: h.activation(
                        linv_b[0:1, :], linv_f[0:1, :], AF.Exp, scale=-1.0),
                        wait=[(se, se.n)], inc=[(se, 1)])
                    m_linv = se.n
                    P.add("ve", lambda h, par=par: h.tensor_copy(
                        hTun[:, :], hpsT[par][0:64, :]),
                        wait=[(te, m_av), (ve, ve.n), (ve, st["hTun_rd"])],
                        inc=[(ve, 1)])
                    m_cp = ve.n
                    return (jj, cc, par, m_linv, m_cp)

                def emit_fin_b(fin):
                    jj, cc, par, m_linv, m_cp = fin
                    for hs in range(2):
                        P.add("te", lambda h, par=par, hs=hs: h.matmul(
                            hpsT[par][0:64, hs * 512:(hs + 1) * 512],
                            ones1[0:1, 0:64], linv_b[0:1, hs * 512:(hs + 1) * 512],
                            start=True, stop=True),
                            wait=([(se, m_linv), (ve, m_cp)] if hs == 0 else []),
                            inc=([(te, 1)] if hs == 1 else []))
                    m_rep = te.n
                    st["linv_rd"] = te.n
                    for hs in range(2):
                        P.add("ve", lambda h, jj=jj, cc=cc, par=par, hs=hs: h.tensor_tensor(
                            hT[hs * 64:hs * 64 + 64, jj * S + cc * CH: jj * S + (cc + 1) * CH],
                            hTun[:, hs * 512:(hs + 1) * 512],
                            hpsT[par][0:64, hs * 512:(hs + 1) * 512], ALU.mult),
                            wait=[(te, m_rep), (ve, ve.n)], inc=[(ve, 1)])
                    st["hTun_rd"] = ve.n
                    st["hps_free"][par] = ve.n
                    return ve.n

                pair_done = {}
                gctr = [0]
                cc_of = {it: divmod(it, 8)[1] for it in range(32)}

                def emit_ag(jj, half):
                    P.add("sy", lambda h, jj=jj, half=half: h.dma_start(
                        bass.AP(ag_in2, (jj * 2 + half) * 128 * MY, [[MY, 128], [1, MY]]),
                        bass.AP(hT, jj * S + half * MY, [[4 * S, 128], [1, MY]])),
                        wait=[(ve, pair_done[(jj, half)])],
                        inc=[(agst[jj * 2 + half], 16)])
                    P.add("ge", lambda h, jj=jj, half=half: h.collective_compute(
                        "AllGather", ALU.bypass,
                        replica_groups=[[0, 1], [2, 3], [4, 5], [6, 7]],
                        ins=[ag_in2[(jj * 2 + half) * 128:(jj * 2 + half + 1) * 128, :].opt()],
                        outs=[ag_outs[jj][half * 256:(half + 1) * 256, :].opt()]),
                        wait=[(agst[jj * 2 + half], 16), (ge, ge.n)],
                        inc=[(ccsj[jj], 1)])

                av_todo = []
                fin_todo = []
                AV_LAG = 2

                def emit_one_av():
                    if av_todo:
                        av_todo.pop(0)()

                def pump(gnow, drain=False):
                    changed = True
                    while changed:
                        changed = False
                        if fin_todo and fin_todo[0][2] <= gnow:
                            stage, payload, _ = fin_todo.pop(0)
                            if stage == "a":
                                pb_ = emit_fin_a(*payload)
                                fin_todo.insert(0, ["b", pb_, gnow + 2])
                            else:
                                jprev, ccprev = payload[0], payload[1]
                                mkv = emit_fin_b(payload)
                                if ccprev == 3:
                                    pair_done[(jprev, 0)] = mkv
                                    emit_ag(jprev, 0)
                                elif ccprev == 7:
                                    pair_done[(jprev, 1)] = mkv
                                    emit_ag(jprev, 1)
                            changed = True
                        if len(av_todo) > (0 if drain else AV_LAG):
                            emit_one_av()
                            changed = True

                for it in range(32):
                    jj, cc = divmod(it, 8)
                    nk = 4 * cc + 4
                    par = it % 2

                    def mk_av(kb, grp, jj=jj, nk=nk, par=par, it=it):
                        def go(kb=kb, grp=grp, jj=jj, nk=nk, par=par, it=it):
                            pbuf, ready = grp[kb]
                            for hs in range(2):
                                w = list(ready) if hs == 0 else []
                                if kb == 0 and hs == 0:
                                    w = w + [(ve, st["hps_free"][par])]
                                hh = 2 * jj + hs
                                P.add("te", lambda h, par=par, hh=hh, kb=kb, pbuf=pbuf,
                                      hs=hs, nk=nk:
                                      h.matmul(hpsT[par][0:65, hs * 512:(hs + 1) * 512],
                                               vv[:, (hh * 32 + kb) * 65:(hh * 32 + kb) * 65 + 65],
                                               probs[pbuf][:, hs * 512:(hs + 1) * 512],
                                               start=(kb == 0), stop=(kb == nk - 1)),
                                      wait=w, inc=([(te, 1)] if hs == 1 else []))
                            st["probs_free"][pbuf] = te.n
                            if kb == nk - 1:
                                fin_todo.append(["a", (jj, cc_of[it], par, te.n),
                                                 gctr[0] + 2])
                            return
                        return go

                    for kb in range(nk):
                        gi = st["gg"]
                        st["gg"] += 1
                        ab = gi % 2
                        pbuf = gi % 6
                        for hs in range(2):
                            pb = 64 * hs
                            P.add("te", lambda h, ab=ab, hs=hs, kb=kb, pb=pb, jj=jj, cc=cc:
                                  h.matmul(sAB[ab][:, hs * 512:(hs + 1) * 512],
                                           kT[pb:pb + 64, jj * S + kb * 128: jj * S + (kb + 1) * 128],
                                           qT[pb:pb + 64, jj * S + cc * CH: jj * S + (cc + 1) * CH],
                                           start=True, stop=True, tile_position=(pb, 0)),
                                  wait=([(se, st["arena_free"][ab])] if hs == 0 else []),
                                  inc=([(te, 1)] if hs == 1 else []))
                        m_sc = te.n
                        P.add("se", lambda h, ab=ab, pbuf=pbuf: h.activation(
                            probs[pbuf][:, :], sAB[ab][:, :], AF.Exp, scale=0.125),
                            wait=[(te, m_sc), (te, st["probs_free"][pbuf])],
                            inc=[(se, 1)])
                        st["arena_free"][ab] = se.n
                        m_exp = se.n
                        grp = {}
                        if kb >= 4 * cc:
                            dlt = kb - 4 * cc
                            P.add("ge", lambda h, pbuf=pbuf, delta=dlt * 128:
                                  h.affine_select(
                                      probs[pbuf][:, 0:512], probs[pbuf][:, 0:512],
                                      [[1, 512]], ALU.is_ge, 0.0,
                                      base=-delta, channel_multiplier=-1),
                                  wait=[(se, m_exp), (ge, m_maskt)], inc=[(ge, 1)])
                            m_g = ge.n
                            P.add("ve", lambda h, pbuf=pbuf, dlt=dlt: h.tensor_mul(
                                probs[pbuf][:, 512:1024], probs[pbuf][:, 512:1024],
                                maskt[:, dlt * 512:(dlt + 1) * 512]),
                                wait=[(se, m_exp), (ve, ve.n), (ge, m_maskt)],
                                inc=[(ve, 1)])
                            grp[kb] = (pbuf, [(ge, m_g), (ve, ve.n)])
                        else:
                            grp[kb] = (pbuf, [(se, m_exp)])
                        av_todo.append(mk_av(kb, grp))
                        gctr[0] += 1
                        pump(gctr[0])
                    if cc >= 4:
                        emit_ffn_step()
                pump(float("inf"), drain=True)
                assert not fin_todo and not av_todo
                while st["ffn_left"]:
                    emit_ffn_step()
                P.add("ve", lambda h: h.tensor_copy(gT[:, :], g_t[:, :]),
                      wait=[(ve, st["ffn_g"]), (ve, ve.n)], inc=[(ve, 1)])
                m_gT_mark = ve.n

                if debug:
                    for nm_, sb in (("qT", qT), ("kT", kT), ("vv", vv), ("s", s_sb),
                                    ("hT", hT), ("gT", gT)):
                        P.add("sy", lambda h, a=dbg[nm_], b=sb: h.dma_start(a.ap(), b[:, :]),
                              wait=[(ve, ve.n), (se, se.n), (te, te.n)],
                              inc=[(os_, 16)])
                    m_dbg = os_.n
                    P.add("sy", lambda h: h.nop(), wait=[(os_, m_dbg)])

        stAttn.close()

        with ExitStack() as stC:
            EC = stC.enter_context
            hT_comb = EC(nc.sbuf_tensor("hT_comb", [128, 8 * MY], BF16))
            xmt = [EC(nc.sbuf_tensor(f"xmt{i}", [128, D], F32)) for i in range(4)]
            out_sb = [EC(nc.sbuf_tensor(f"out_sb{i}", [128, D], F32)) for i in range(2)]

            gs_h = gs.h

            def mk_readback(jjs):
                tag = "".join(str(j) for j in jjs)
                def emit(h, tag=tag):
                    with h.register("qr" + tag) as qr, h.register("qs" + tag) as qr2:
                        h.reg_load(qr, qoff_sb[0:1, 0:1])
                        h.reg_mul(qr, qr, 256)
                        last = None
                        for jj in jjs:
                            for r2 in range(2):
                                h.reg_mov(qr2, r2 * 128 * MY)
                                h.reg_add(qr2, qr2, qr)
                                rb = r2 * 4 + jj
                                inst = h.dma_start(
                                    bass.AP(hT_comb, rb * MY, [[8 * MY, 128], [1, MY]]),
                                    bass.AP(ag_outs[jj], qr2, [[MY, 128], [1, MY]]))
                                if last is not None:
                                    last.then_inc(gs_h, 16)
                                last = inst
                        return last
                return emit
            mV2, mT2, mS2 = ve.n, te.n, se.n
            P.add("ge", mk_readback([0, 1, 2]),
                  wait=[(ccsj[0], 2), (ccsj[1], 2), (ccsj[2], 2), (cs, 16 * 8),
                        (ve, mV2)],
                  inc=[(gs, 16)])
            gs.n += 5 * 16
            P.add("ge", mk_readback([3]), wait=[(ccsj[3], 2)], inc=[(gs, 16)])
            gs.n += 1 * 16
            m_rb = gs.n
            xm_marks = {}
            for tb in range(4):
                P.add("sy", lambda h, a=xmt[tb], tb=tb: h.dma_start(
                    a[:, :], xm[tb * 128:(tb + 1) * 128, :]),
                    wait=[(ve, mV2), (te, mT2), (se, mS2)],
                    inc=[(xsb[tb % 4], 16)])
                xm_marks[tb] = (xsb[tb % 4], xsb[tb % 4].n)

            with ExitStack() as stPs4:
                E4 = stPs4.enter_context
                ops = [E4(nc.psum_tensor(f"ops{i}", [128, 512], F32)) for i in range(4)]
                obank_free = [(ve, 0)] * 4
                xm_free = [0, 0, 0, 0]
                osb_free = [(osb2[0], 0), (osb2[1], 0)]
                m_gT = ve.n

                for tb in range(16):
                    xb = tb % 4
                    ob = tb % 2
                    if tb >= 4:
                        P.add("sy", lambda h, a=xmt[xb], tb=tb: h.dma_start(
                            a[:, :], xm[tb * 128:(tb + 1) * 128, :]),
                            wait=[(ve, xm_free[xb])], inc=[(xsb[xb], 16)])
                        xm_marks[tb] = (xsb[xb], xsb[xb].n)
                    m_xm = xm_marks[tb]
                    m_nn = [0, 0]
                    for nn in range(2):
                        bk = (2 * tb + nn) % 4
                        for rb in range(8):
                            P.add("te", lambda h, bk=bk, rb=rb, tb=tb, nn=nn: h.matmul(
                                ops[bk][:, :],
                                hT_comb[:, rb * MY + tb * 128: rb * MY + (tb + 1) * 128],
                                wo_bf[:, rb * D + nn * 512: rb * D + (nn + 1) * 512],
                                start=(rb == 0), stop=False),
                                wait=([(gs, m_rb), obank_free[bk], wmarks["wo"]]
                                      if rb == 0 else []))
                        P.add("te", lambda h, bk=bk, tb=tb, nn=nn: h.matmul(
                            ops[bk][:, :],
                            gT[0:64, tb * 128:(tb + 1) * 128],
                            wout_bf[0:64, nn * 512:(nn + 1) * 512],
                            start=False, stop=False),
                            wait=[(ve, m_gT), wmarks["wout"]])
                        P.add("te", lambda h, bk=bk, nn=nn: h.matmul(
                            ops[bk][:, :], ones1[0:1, 0:128],
                            bo_sb[0:1, nn * 512:(nn + 1) * 512],
                            start=False, stop=False))
                        P.add("te", lambda h, bk=bk, nn=nn: h.matmul(
                            ops[bk][:, :], ones1[0:1, 0:128],
                            bout_sb[0:1, nn * 512:(nn + 1) * 512],
                            start=False, stop=True),
                            inc=[(te, 1)])
                        m_mm = te.n
                        P.add("ve", lambda h, bk=bk, ob=ob, xb=xb, nn=nn: h.tensor_tensor(
                            out_sb[ob][:, nn * 512:(nn + 1) * 512],
                            xmt[xb][:, nn * 512:(nn + 1) * 512],
                            ops[bk][:, :], ALU.add),
                            wait=[(te, m_mm), m_xm, osb_free[ob],
                                  (ve, ve.n)],
                            inc=[(ve, 1)])
                        obank_free[bk] = (ve, ve.n)
                        m_nn[nn] = ve.n
                    xm_free[xb] = ve.n
                    P.add("sy", lambda h, ob=ob, tb=tb: h.dma_start(
                        out[tb * 128:(tb + 1) * 128, :], out_sb[ob][:, :]),
                        wait=[(ve, m_nn[1])], inc=[(osb2[ob], 16)])
                    osb_free[ob] = (osb2[ob], osb2[ob].n)
                P.add("sy", lambda h: h.nop(),
                      wait=[(osb2[0], osb2[0].n), (osb2[1], osb2[1].n)])
                P.replay_into(nc)
        stB2.close()

    return nc, dbg



def _shard_inputs(inp):
    f32 = np.float32
    x = np.asarray(inp["L0_attn_x"], f32)
    Wq = np.asarray(inp["L0_attn_Wq"], f32)
    bq = np.asarray(inp["L0_attn_bq"], f32)
    Wk = np.asarray(inp["L0_attn_Wk"], f32)
    bk = np.asarray(inp["L0_attn_bk"], f32)
    Wv = np.asarray(inp["L0_attn_Wv"], f32)
    bv = np.asarray(inp["L0_attn_bv"], f32)
    Wo = np.asarray(inp["L0_attn_Wo"], f32)
    bo = np.asarray(inp["L0_attn_bo"], f32)
    Win = np.asarray(inp["L0_ffn_Win"], f32)
    bin_ = np.asarray(inp["L0_ffn_bin_"], f32)
    W1 = np.asarray(inp["L0_ffn_W1"], f32)
    b1 = np.asarray(inp["L0_ffn_b1"], f32)
    W2 = np.asarray(inp["L0_ffn_W2"], f32)
    b2 = np.asarray(inp["L0_ffn_b2"], f32)
    Wout = np.asarray(inp["L0_ffn_Wout"], f32)
    bout = np.asarray(inp["L0_ffn_bout"], f32)

    C = np.ascontiguousarray
    in_maps = []
    for c in range(8):
        b, r = divmod(c, 2)
        sl = slice(r * 512, (r + 1) * 512)
        in_maps.append({
            "xf": C(x[b]),
            "xm": C(x[b, r * MY:(r + 1) * MY]),
            "wq": C(Wq[:, sl]),
            "wk": C(Wk[:, sl]),
            "wv": C(Wv[:, sl]),
            "wo": C(Wo),
            "bqh": C(bq[sl].reshape(4, 128).T),
            "bkh": C(bk[sl].reshape(4, 128).T),
            "bvh": C(bv[sl].reshape(1, 512)),
            "boh": C(bo.reshape(1, D)),
            "winh": C(Win),
            "binh": C(bin_.reshape(64, 1)),
            "w1h": C(W1),
            "b1h": C(b1),
            "w2h": C(W2),
            "b2h": C(b2.reshape(64, 1)),
            "wouth": C(Wout),
            "bouth": C(bout.reshape(1, D)),
            "qoff": np.array([[r * MY]], dtype=np.int32),
        })
    return in_maps


def kernel(**inputs):
    in_maps = _shard_inputs(inputs)
    nc, _ = build(debug=False)
    res = run_bass_kernel_spmd(nc, in_maps, core_ids=list(range(8)))
    out = np.empty((4, S, D), np.float32)
    for c in range(8):
        b, r = divmod(c, 2)
        out[b, r * MY:(r + 1) * MY] = np.asarray(res.results[c]["out"])
    return out


# version 24
# speedup vs baseline: 1.0273x; 1.0273x over previous
import numpy as np
import ml_dtypes
from contextlib import ExitStack
import concourse.bass as bass
import concourse.mybir as mybir
from concourse.bass_utils import run_bass_kernel_spmd

F32 = mybir.dt.float32
BF16 = mybir.dt.bfloat16
I32 = mybir.dt.int32
AF = mybir.ActivationFunctionType
ALU = mybir.AluOpType
AX = mybir.AxisListType

D = 1024
S = 4096
MY = 2048
CH = 512
NCH = S // CH
NMYCH = MY // CH
INV_D = 1.0 / D
EPS = 1e-5


class Sem:
    __slots__ = ("h", "n")
    def __init__(self, h):
        self.h = h
        self.n = 0


class Plan:
    ENGS = ("te", "ve", "se", "ge", "sy")

    def __init__(self):
        self.q = {e: [] for e in self.ENGS}

    def add(self, eng, emit, wait=(), inc=()):
        best = {}
        for s, v in wait:
            if v > best.get(id(s), (None, -1))[1]:
                best[id(s)] = (s, v)
        self.q[eng].append((tuple(best.values()), emit, tuple(inc)))
        for s, k in inc:
            s.n += k

    def replay_into(self, nc):
        qs = self.q
        with nc.Block() as blk:
            if qs["te"]:
                blk.tensor(lambda h: Plan._run(h, qs["te"]))
            if qs["ve"]:
                blk.vector(lambda h: Plan._run(h, qs["ve"]))
            if qs["se"]:
                blk.scalar(lambda h: Plan._run(h, qs["se"]))
            if qs["ge"]:
                blk.gpsimd(lambda h: Plan._run(h, qs["ge"]))
            if qs["sy"]:
                blk.sync(lambda h: Plan._run(h, qs["sy"]))
        self.q = {e: [] for e in self.ENGS}

    @staticmethod
    def _run(h, q):
        for wait, emit, inc in q:
            for s, v in wait:
                if v > 0:
                    h.wait_ge(s.h, v)
            inst = emit(h)
            for s, k in inc:
                inst.then_inc(s.h, k)


def build(debug=False):
    nc = bass.Bass("TRN2", target_bir_lowering=False)

    xf = nc.dram_tensor("xf", [S, D], F32, kind="ExternalInput")
    xm = nc.dram_tensor("xm", [MY, D], F32, kind="ExternalInput")
    wq = nc.dram_tensor("wq", [D, 512], F32, kind="ExternalInput")
    wk = nc.dram_tensor("wk", [D, 512], F32, kind="ExternalInput")
    wv = nc.dram_tensor("wv", [D, 512], F32, kind="ExternalInput")
    wo = nc.dram_tensor("wo", [D, D], F32, kind="ExternalInput")
    bqh = nc.dram_tensor("bqh", [128, 4], F32, kind="ExternalInput")
    bkh = nc.dram_tensor("bkh", [128, 4], F32, kind="ExternalInput")
    bvh = nc.dram_tensor("bvh", [1, 512], F32, kind="ExternalInput")
    boh = nc.dram_tensor("boh", [1, D], F32, kind="ExternalInput")
    winh = nc.dram_tensor("winh", [D, 64], F32, kind="ExternalInput")
    binh = nc.dram_tensor("binh", [64, 1], F32, kind="ExternalInput")
    w1h = nc.dram_tensor("w1h", [64, 16], F32, kind="ExternalInput")
    b1h = nc.dram_tensor("b1h", [64, 16], F32, kind="ExternalInput")
    w2h = nc.dram_tensor("w2h", [64, 16], F32, kind="ExternalInput")
    b2h = nc.dram_tensor("b2h", [64, 1], F32, kind="ExternalInput")
    wouth = nc.dram_tensor("wouth", [64, D], F32, kind="ExternalInput")
    bouth = nc.dram_tensor("bouth", [1, D], F32, kind="ExternalInput")
    qoff = nc.dram_tensor("qoff", [1, 1], I32, kind="ExternalInput")
    out = nc.dram_tensor("out", [MY, D], F32, kind="ExternalOutput")
    ag_in2 = nc.dram_tensor("ag_in2", [1024, MY], BF16)
    ag_outs = [nc.dram_tensor(f"ag_out{j}", [512, MY], BF16) for j in range(4)]

    dbg = {}
    if debug:
        dbg["qT"] = nc.dram_tensor("d_qT", [128, 4 * S], BF16, kind="ExternalOutput")
        dbg["kT"] = nc.dram_tensor("d_kT", [128, 4 * S], BF16, kind="ExternalOutput")
        dbg["vv"] = nc.dram_tensor("d_vv", [128, 8 * 32 * 65], BF16, kind="ExternalOutput")
        dbg["s"] = nc.dram_tensor("d_s", [64, MY], F32, kind="ExternalOutput")
        dbg["hT"] = nc.dram_tensor("d_hT", [128, 4 * S], BF16, kind="ExternalOutput")
        dbg["gT"] = nc.dram_tensor("d_gT", [64, MY], BF16, kind="ExternalOutput")

    with ExitStack() as stP:
        EP = stP.enter_context
        stAttn = ExitStack()
        qT = stAttn.enter_context(nc.sbuf_tensor("qT", [128, 4 * S], BF16, side="right"))
        kT = stAttn.enter_context(nc.sbuf_tensor("kT", [128, 4 * S], BF16, side="right"))
        vv = stAttn.enter_context(nc.sbuf_tensor("vv", [128, 8 * 32 * 65], BF16, side="right"))
        s_sb = stAttn.enter_context(nc.sbuf_tensor("s_sb", [64, MY], F32, side="right"))
        gT = EP(nc.sbuf_tensor("gT", [64, MY], BF16))
        ident = EP(nc.sbuf_tensor("ident", [128, 128], BF16))
        ones1 = EP(nc.sbuf_tensor("ones1", [1, 128], BF16))
        bq_sb = EP(nc.sbuf_tensor("bq_sb", [128, 4], F32))
        bk_sb = EP(nc.sbuf_tensor("bk_sb", [128, 4], F32))
        bv_sb = EP(nc.sbuf_tensor("bv_sb", [1, 512], BF16))
        bo_sb = EP(nc.sbuf_tensor("bo_sb", [1, D], BF16))
        bout_sb = EP(nc.sbuf_tensor("bout_sb", [1, D], BF16))
        bin_sb = EP(nc.sbuf_tensor("bin_sb", [64, 1], F32))
        w1_sb = EP(nc.sbuf_tensor("w1_sb", [64, 16], F32))
        b1_sb = EP(nc.sbuf_tensor("b1_sb", [64, 16], F32))
        w2_sb = EP(nc.sbuf_tensor("w2_sb", [64, 16], F32))
        b2_sb = EP(nc.sbuf_tensor("b2_sb", [64, 1], F32))
        qoff_sb = EP(nc.sbuf_tensor("qoff_sb", [1, 1], I32))
        eps_sb = EP(nc.sbuf_tensor("eps_sb", [128, 1], F32))

        te = Sem(EP(nc.semaphore("te_s")))
        ve = Sem(EP(nc.semaphore("ve_s")))
        se = Sem(EP(nc.semaphore("se_s")))
        ge = Sem(EP(nc.semaphore("ge_s")))
        cs = Sem(EP(nc.semaphore("cs_s")))
        xsb = [Sem(EP(nc.semaphore(f"xsb{i}_s"))) for i in range(4)]
        wsg = {k: Sem(EP(nc.semaphore(f"ws_{k}_s")))
               for k in ("wc", "wq", "wk", "wv", "win", "wo", "wout")}
        gs = Sem(EP(nc.semaphore("gs_s")))
        agst = [Sem(EP(nc.semaphore(f"agst{i}_s"))) for i in range(8)]
        ccsj = [Sem(EP(nc.semaphore(f"ccsj{i}_s"))) for i in range(4)]
        osb2 = [Sem(EP(nc.semaphore(f"osb2{i}_s"))) for i in range(2)]
        os_ = Sem(EP(nc.semaphore("os_s")))

        P = Plan()

        for src, dst in ((bqh, bq_sb), (bkh, bk_sb), (binh, bin_sb),
                         (w1h, w1_sb), (b1h, b1_sb), (w2h, w2_sb), (b2h, b2_sb),
                         (qoff, qoff_sb)):
            P.add("sy", lambda h, a=dst, b=src: h.dma_start(a[:, :], b.ap()),
                  inc=[(cs, 16)])
        for src, dst in ((bvh, bv_sb), (boh, bo_sb), (bouth, bout_sb)):
            P.add("ge", lambda h, a=dst, b=src: h.dma_start(a[:, :], b.ap()),
                  inc=[(wsg["wc"], 16)])
        m_bv = wsg["wc"].n
        P.replay_into(nc)

        with ExitStack() as stA:
            EA = stA.enter_context
            wq_bf = EA(nc.sbuf_tensor("wq_bf", [128, 8 * 512], BF16, side="right"))
            wk_bf = EA(nc.sbuf_tensor("wk_bf", [128, 8 * 512], BF16, side="right"))
            wv_bf = EA(nc.sbuf_tensor("wv_bf", [128, 8 * 512], BF16, side="right"))
            win_bf = EA(nc.sbuf_tensor("win_bf", [128, 8 * 64], BF16, side="right"))
            x_t = [EA(nc.sbuf_tensor(f"x_t{i}", [128, D], F32, side="right")) for i in range(4)]
            xn_t = [EA(nc.sbuf_tensor(f"xn_t{i}", [128, D], BF16, side="right")) for i in range(4)]
            xnT = [EA(nc.sbuf_tensor(f"xnT{i}", [128, 8 * CH], BF16, side="right")) for i in range(2)]
            sx = EA(nc.sbuf_tensor("sx", [128, 8], F32, side="right"))
            ssq = EA(nc.sbuf_tensor("ssq", [128, 8], F32, side="right"))
            nm = EA(nc.sbuf_tensor("nm", [128, 8], F32, side="right"))
            var = EA(nc.sbuf_tensor("var", [128, 8], F32, side="right"))
            m2t = EA(nc.sbuf_tensor("m2t", [128, 8], F32, side="right"))
            tln = EA(nc.sbuf_tensor("tln", [128, 8], F32, side="right"))
            rstd = EA(nc.sbuf_tensor("rstd", [128, 8], F32, side="right"))

            with ExitStack() as stPs1:
                E1 = stPs1.enter_context
                ps_tr = [E1(nc.psum_tensor(f"ps_tr{i}", [128, 1024], BF16)) for i in range(2)]
                ps_p = [E1(nc.psum_tensor(f"ps_p{i}", [128, 512], F32)) for i in range(4)]

                P.add("ge", lambda h: h.iota(ident[:, :], [[1, 128]], base=0,
                                             channel_multiplier=-1,
                                             allow_small_or_imprecise_dtypes=True),
                      inc=[(ge, 1)])
                m_iota = ge.n
                P.add("ve", lambda h: h.tensor_single_scalar(
                    ident[:, :], ident[:, :], 0.0, ALU.is_equal),
                    wait=[(ge, m_iota)], inc=[(ve, 1)])
                P.add("ve", lambda h: h.memset(ones1[:, :], 1.0),
                      wait=[(ve, ve.n)], inc=[(ve, 1)])
                P.add("ve", lambda h: h.memset(
                    bass.AP(vv, 64, [[8 * 32 * 65, 128], [65, 8 * 32], [1, 1]]), 1.0),
                    wait=[(ve, ve.n)], inc=[(ve, 1)])
                P.add("ve", lambda h: h.memset(eps_sb[:, :], EPS),
                      wait=[(ve, ve.n)], inc=[(ve, 1)])
                m_const = ve.n

                wmarks = {}
                for name, wsrc, wdst, nblk, bw in (
                        ("wq", wq, wq_bf, 8, 512), ("wk", wk, wk_bf, 8, 512),
                        ("wv", wv, wv_bf, 8, 512), ("win", winh, win_bf, 8, 64)):
                    for db in range(nblk):
                        P.add("ge", lambda h, a=wdst, b=wsrc, db=db, bw=bw, name=name:
                              h.dma_start(a[:, db * bw:(db + 1) * bw],
                                          b[db * 128:(db + 1) * 128, :]),
                              inc=[(wsg[name], 16)])
                    wmarks[name] = (wsg[name], wsg[name].n)

                xt_free = [0, 0, 0, 0]
                xn_free = [0, 0, 0, 0]
                tr_free = [0, 0]
                bank_free = [(ve, 0)] * 4
                ch_done = {}
                bank_i = 0

                for c in range(NCH + NMYCH):
                    bc = c % 2
                    sc4 = (c % 2) * 4
                    full = c < NCH
                    for half in range(2):
                        pair = (2 * half, 2 * half + 1)
                        last_sum = 0
                        for ti in pair:
                            xb = ti
                            col = sc4 + ti
                            if full:
                                src = xf[c * CH + ti * 128: c * CH + (ti + 1) * 128, :]
                            else:
                                base = (c - NCH) * CH + ti * 128
                                src = xm[base: base + 128, :]
                            P.add("sy", lambda h, a=x_t[xb], b=src: h.dma_start(a[:, :], b),
                                  wait=[(ve, xt_free[xb])], inc=[(xsb[xb], 16)])
                            m_x = (xsb[xb], xsb[xb].n)
                            P.add("se", lambda h, a=x_t[xb], o=xn_t[ti], col=col:
                                h.activation(
                                    o[:, :], a[:, :], AF.Identity,
                                    accum_out=sx[:, col:col + 1]),
                                wait=[m_x, (se, se.n), (te, xn_free[ti])],
                                inc=[(se, 1)])
                            last_sum = se.n
                            P.add("se", lambda h, a=x_t[xb], o=xn_t[ti], col=col:
                                h.activation(
                                    o[:, :], a[:, :], AF.Square,
                                    accum_out=ssq[:, col:col + 1]),
                                wait=[m_x, (se, se.n), (te, xn_free[ti])],
                                inc=[(se, 1)])
                            last_sq = se.n
                        s2 = sc4 + 2 * half
                        P.add("ve", lambda h, s2=s2: h.tensor_scalar(
                            nm[:, s2:s2 + 2], sx[:, s2:s2 + 2], -INV_D, None, ALU.mult),
                            wait=[(se, last_sum), (ve, ve.n)], inc=[(ve, 1)])
                        P.add("ve", lambda h, s2=s2: h.tensor_mul(
                            m2t[:, s2:s2 + 2], nm[:, s2:s2 + 2], nm[:, s2:s2 + 2]),
                            wait=[(ve, ve.n)], inc=[(ve, 1)])
                        P.add("ve", lambda h, s2=s2: h.scalar_tensor_tensor(
                            var[:, s2:s2 + 2], ssq[:, s2:s2 + 2], INV_D, m2t[:, s2:s2 + 2],
                            ALU.mult, ALU.subtract),
                            wait=[(ve, ve.n), (se, last_sq)], inc=[(ve, 1)])
                        m_var = ve.n
                        P.add("se", lambda h, s2=s2: h.activation(
                            tln[:, s2:s2 + 2], var[:, s2:s2 + 2], AF.Ln,
                            bias=eps_sb[:, 0:1]),
                            wait=[(ve, m_var), (se, se.n), (ve, m_const)], inc=[(se, 1)])
                        P.add("se", lambda h, s2=s2: h.activation(
                            rstd[:, s2:s2 + 2], tln[:, s2:s2 + 2], AF.Exp, scale=-0.5),
                            wait=[(se, se.n)], inc=[(se, 1)])
                        m_rstd = se.n
                        m_trs = {}
                        for ti in pair:
                            xb = ti
                            col = sc4 + ti
                            P.add("ve", lambda h, a=xn_t[ti], b=x_t[xb], col=col: h.tensor_scalar(
                                a[:, :], b[:, :], nm[:, col:col + 1], rstd[:, col:col + 1],
                                ALU.add, ALU.mult),
                                wait=[(se, m_rstd), (te, xn_free[ti]), (ve, ve.n)],
                                inc=[(ve, 1)])
                            xt_free[xb] = ve.n
                            m_xn = ve.n
                            tb = ti % 2
                            for db in range(8):
                                P.add("te", lambda h, a=ps_tr[tb], b=xn_t[ti], db=db: h.transpose(
                                    a[:, db * 128:(db + 1) * 128],
                                    b[:, db * 128:(db + 1) * 128], ident[:, :]),
                                    wait=([(ve, m_xn), (ve, tr_free[tb]), (ve, m_const)]
                                          if db == 0 else []),
                                    inc=([(te, 1)] if db == 7 else []))
                            xn_free[ti] = te.n
                            m_trs[ti] = te.n
                        for ti in pair:
                            tb = ti % 2
                            P.add("ve", lambda h, a=xnT[bc], b=ps_tr[tb], ti=ti: h.tensor_copy(
                                bass.AP(a, ti * 128, [[8 * CH, 128], [CH, 8], [1, 128]]),
                                b[:, :]),
                                wait=[(te, m_trs[ti]), (te, ch_done.get(c - 2, 0)), (ve, ve.n)],
                                inc=[(ve, 1)])
                            tr_free[tb] = ve.n
                    m_evac = ve.n
                    if full:
                        for wb, bias_t, dst, wmk in ((wq_bf, bq_sb, qT, wmarks["wq"]),
                                                     (wk_bf, bk_sb, kT, wmarks["wk"])):
                            for j in range(4):
                                bk = bank_i % 4
                                bank_i += 1
                                for db in range(8):
                                    P.add("te", lambda h, bk=bk, wb=wb, db=db, j=j, bc=bc:
                                          h.matmul(ps_p[bk][:, :],
                                                   wb[:, db * 512 + j * 128: db * 512 + (j + 1) * 128],
                                                   xnT[bc][:, db * CH:(db + 1) * CH],
                                                   start=(db == 0), stop=(db == 7)),
                                          wait=([(ve, m_evac), bank_free[bk], wmk]
                                                if db == 0 else []),
                                          inc=([(te, 1)] if db == 7 else []))
                                m_mm = te.n
                                P.add("se", lambda h, bk=bk, dst=dst, bias_t=bias_t, j=j, c=c:
                                      h.activation(dst[:, j * S + c * CH: j * S + (c + 1) * CH],
                                                   ps_p[bk][:, :], AF.Identity,
                                                   bias=bias_t[:, j:j + 1]),
                                      wait=[(se, se.n), (te, m_mm), (cs, 16 * 8)],
                                      inc=[(se, 1)])
                                bank_free[bk] = (se, se.n)
                        for ti in range(4):
                            bk = bank_i % 4
                            bank_i += 1
                            for db in range(8):
                                P.add("te", lambda h, bk=bk, db=db, ti=ti, bc=bc:
                                      h.matmul(ps_p[bk][:, :],
                                               xnT[bc][:, db * CH + ti * 128: db * CH + (ti + 1) * 128],
                                               wv_bf[:, db * 512:(db + 1) * 512],
                                               start=(db == 0), stop=False),
                                      wait=([(ve, m_evac), bank_free[bk],
                                             wmarks["wv"]] if db == 0 else []))
                            P.add("te", lambda h, bk=bk: h.matmul(
                                ps_p[bk][:, :], ones1[0:1, 0:128], bv_sb[0:1, :],
                                start=False, stop=True),
                                wait=[(wsg["wc"], m_bv)], inc=[(te, 1)])
                            m_mm = te.n
                            tb_idx = c * 4 + ti
                            P.add("ve", lambda h, bk=bk, tb_idx=tb_idx: h.tensor_copy(
                                bass.AP(vv, tb_idx * 65,
                                        [[8 * 32 * 65, 128], [32 * 65, 8], [1, 64]]),
                                ps_p[bk][:, :]),
                                wait=[(te, m_mm), (ve, ve.n)], inc=[(ve, 1)])
                            bank_free[bk] = (ve, ve.n)
                    else:
                        bk = bank_i % 4
                        bank_i += 1
                        for db in range(8):
                            P.add("te", lambda h, bk=bk, db=db, bc=bc: h.matmul(
                                ps_p[bk][0:64, :],
                                win_bf[:, db * 64:(db + 1) * 64],
                                xnT[bc][:, db * CH:(db + 1) * CH],
                                start=(db == 0), stop=(db == 7)),
                                wait=([(ve, m_evac), bank_free[bk], wmarks["win"]]
                                      if db == 0 else []),
                                inc=([(te, 1)] if db == 7 else []))
                        m_mm = te.n
                        cc = c - NCH
                        P.add("se", lambda h, bk=bk, cc=cc: h.activation(
                            s_sb[0:64, cc * CH:(cc + 1) * CH], ps_p[bk][0:64, :],
                            AF.Identity, bias=bin_sb[0:64, 0:1]),
                            wait=[(se, se.n), (te, m_mm)], inc=[(se, 1)])
                        bank_free[bk] = (se, se.n)
                    ch_done[c] = te.n

                P.replay_into(nc)

        stB2 = ExitStack()
        hT = stB2.enter_context(nc.sbuf_tensor("hT", [128, 4 * S], BF16))
        wo_bf = stB2.enter_context(nc.sbuf_tensor("wo_bf", [128, 8 * D], BF16))
        wout_bf = stB2.enter_context(nc.sbuf_tensor("wout_bf", [64, D], BF16))
        for db in range(8):
            P.add("ge", lambda h, db=db: h.dma_start(
                wo_bf[:, db * D:(db + 1) * D], wo[db * 128:(db + 1) * 128, :]),
                inc=[(wsg["wo"], 16)])
        wmarks["wo"] = (wsg["wo"], wsg["wo"].n)
        P.add("ge", lambda h: h.dma_start(wout_bf[:, :], wouth.ap()),
              inc=[(wsg["wout"], 16)])
        wmarks["wout"] = (wsg["wout"], wsg["wout"].n)
        with ExitStack() as stB:
            EB = stB.enter_context
            probs = [EB(nc.sbuf_tensor(f"probs{i}", [128, 1024], BF16, side="right")) for i in range(6)]
            hTun = EB(nc.sbuf_tensor("hTun", [64, 1024], BF16, side="right"))
            linv_f = EB(nc.sbuf_tensor("linv_f", [1, 1024], F32, side="right"))
            linv_b = EB(nc.sbuf_tensor("linv_b", [1, 1024], BF16, side="right"))
            h_t = EB(nc.sbuf_tensor("h_t", [64, MY], F32, side="right"))
            g_t = EB(nc.sbuf_tensor("g_t", [64, MY], F32, side="right"))
            maskt = EB(nc.sbuf_tensor("maskt", [128, 4 * 512], BF16, side="right"))
            with ExitStack() as stPs2:
                E2 = stPs2.enter_context
                sAB = [E2(nc.psum_tensor(f"sAB{i}", [128, 1024], F32)) for i in range(2)]
                hpsT = [E2(nc.psum_tensor(f"hpsT{i}", [65, 1024], F32)) for i in range(2)]

                P.add("ve", lambda h: h.memset(maskt[:, :], 1.0), wait=[(ve, ve.n)],
                      inc=[(ve, 1)])
                m_ones = ve.n
                for dlt in range(4):
                    P.add("ge", lambda h, dlt=dlt: h.affine_select(
                        maskt[:, dlt * 512:(dlt + 1) * 512],
                        maskt[:, dlt * 512:(dlt + 1) * 512],
                        [[1, 512]], ALU.is_ge, 0.0,
                        base=-dlt * 128, channel_multiplier=-1),
                        wait=[(ve, m_ones), (ge, ge.n)], inc=[(ge, 1)])
                m_maskt = ge.n

                st = {
                    "arena_free": [0, 0],
                    "probs_free": [0] * 6,
                    "hps_free": [0, 0],
                    "linv_rd": 0,
                    "hTun_rd": 0,
                    "gg": 0,
                    "ffn_g": 0,
                    "ffn_left": list(range(16)),
                }

                def emit_ffn_step():
                    if not st["ffn_left"]:
                        return
                    i = st["ffn_left"].pop(0)
                    P.add("ve", lambda h, i=i: h.tensor_scalar(
                        h_t[:, :], s_sb[:, :], w1_sb[:, i:i + 1], b1_sb[:, i:i + 1],
                        ALU.mult, ALU.add),
                        wait=[(ve, st["ffn_g"]), (ve, ve.n)], inc=[(ve, 1)])
                    P.add("ve", lambda h: h.tensor_scalar_max(h_t[:, :], h_t[:, :], 0.0),
                          wait=[(ve, ve.n)], inc=[(ve, 1)])
                    m_h = ve.n
                    if i == 0:
                        P.add("ve", lambda h: h.tensor_scalar(
                            g_t[:, :], h_t[:, :], w2_sb[:, 0:1], b2_sb[:, 0:1],
                            ALU.mult, ALU.add),
                            wait=[(ve, m_h)], inc=[(ve, 1)])
                    else:
                        P.add("ve", lambda h, i=i: h.scalar_tensor_tensor(
                            g_t[:, :], h_t[:, :], w2_sb[:, i:i + 1], g_t[:, :],
                            ALU.mult, ALU.add),
                            wait=[(ve, m_h)], inc=[(ve, 1)])
                    st["ffn_g"] = ve.n

                def emit_fin_a(jj, cc, par, m_av):
                    P.add("se", lambda h, par=par: h.activation(
                        linv_f[0:1, :], hpsT[par][64:65, :], AF.Ln),
                        wait=[(te, m_av), (se, se.n), (te, st["linv_rd"])],
                        inc=[(se, 1)])
                    P.add("se", lambda h: h.activation(
                        linv_b[0:1, :], linv_f[0:1, :], AF.Exp, scale=-1.0),
                        wait=[(se, se.n)], inc=[(se, 1)])
                    m_linv = se.n
                    P.add("ve", lambda h, par=par: h.tensor_copy(
                        hTun[:, :], hpsT[par][0:64, :]),
                        wait=[(te, m_av), (ve, ve.n), (ve, st["hTun_rd"])],
                        inc=[(ve, 1)])
                    m_cp = ve.n
                    return (jj, cc, par, m_linv, m_cp)

                def emit_fin_b(fin):
                    jj, cc, par, m_linv, m_cp = fin
                    for hs in range(2):
                        P.add("te", lambda h, par=par, hs=hs: h.matmul(
                            hpsT[par][0:64, hs * 512:(hs + 1) * 512],
                            ones1[0:1, 0:64], linv_b[0:1, hs * 512:(hs + 1) * 512],
                            start=True, stop=True),
                            wait=([(se, m_linv), (ve, m_cp)] if hs == 0 else []),
                            inc=([(te, 1)] if hs == 1 else []))
                    m_rep = te.n
                    st["linv_rd"] = te.n
                    for hs in range(2):
                        P.add("ve", lambda h, jj=jj, cc=cc, par=par, hs=hs: h.tensor_tensor(
                            hT[hs * 64:hs * 64 + 64, jj * S + cc * CH: jj * S + (cc + 1) * CH],
                            hTun[:, hs * 512:(hs + 1) * 512],
                            hpsT[par][0:64, hs * 512:(hs + 1) * 512], ALU.mult),
                            wait=[(te, m_rep), (ve, ve.n)], inc=[(ve, 1)])
                    st["hTun_rd"] = ve.n
                    st["hps_free"][par] = ve.n
                    return ve.n

                pair_done = {}
                gctr = [0]
                cc_of = {it: divmod(it, 8)[1] for it in range(32)}

                def emit_ag(jj, half):
                    P.add("sy", lambda h, jj=jj, half=half: h.dma_start(
                        bass.AP(ag_in2, (jj * 2 + half) * 128 * MY, [[MY, 128], [1, MY]]),
                        bass.AP(hT, jj * S + half * MY, [[4 * S, 128], [1, MY]])),
                        wait=[(ve, pair_done[(jj, half)])],
                        inc=[(agst[jj * 2 + half], 16)])
                    P.add("ge", lambda h, jj=jj, half=half: h.collective_compute(
                        "AllGather", ALU.bypass,
                        replica_groups=[[0, 1], [2, 3], [4, 5], [6, 7]],
                        ins=[ag_in2[(jj * 2 + half) * 128:(jj * 2 + half + 1) * 128, :].opt()],
                        outs=[ag_outs[jj][half * 256:(half + 1) * 256, :].opt()]),
                        wait=[(agst[jj * 2 + half], 16), (ge, ge.n)],
                        inc=[(ccsj[jj], 1)])

                av_todo = []
                fin_todo = []
                AV_LAG = 4

                def emit_one_av():
                    if av_todo:
                        av_todo.pop(0)()

                def pump(gnow, drain=False):
                    changed = True
                    while changed:
                        changed = False
                        if fin_todo and fin_todo[0][2] <= gnow:
                            stage, payload, _ = fin_todo.pop(0)
                            if stage == "a":
                                pb_ = emit_fin_a(*payload)
                                fin_todo.insert(0, ["b", pb_, gnow + 2])
                            else:
                                jprev, ccprev = payload[0], payload[1]
                                mkv = emit_fin_b(payload)
                                if ccprev == 3:
                                    pair_done[(jprev, 0)] = mkv
                                    emit_ag(jprev, 0)
                                elif ccprev == 7:
                                    pair_done[(jprev, 1)] = mkv
                                    emit_ag(jprev, 1)
                            changed = True
                        if len(av_todo) > (0 if drain else AV_LAG):
                            emit_one_av()
                            changed = True

                for it in range(32):
                    jj, cc = divmod(it, 8)
                    nk = 4 * cc + 4
                    par = it % 2

                    def mk_av(kb, grp, jj=jj, nk=nk, par=par, it=it):
                        def go(kb=kb, grp=grp, jj=jj, nk=nk, par=par, it=it):
                            pbuf, ready = grp[kb]
                            for hs in range(2):
                                w = list(ready) if hs == 0 else []
                                if kb == 0 and hs == 0:
                                    w = w + [(ve, st["hps_free"][par])]
                                hh = 2 * jj + hs
                                P.add("te", lambda h, par=par, hh=hh, kb=kb, pbuf=pbuf,
                                      hs=hs, nk=nk:
                                      h.matmul(hpsT[par][0:65, hs * 512:(hs + 1) * 512],
                                               vv[:, (hh * 32 + kb) * 65:(hh * 32 + kb) * 65 + 65],
                                               probs[pbuf][:, hs * 512:(hs + 1) * 512],
                                               start=(kb == 0), stop=(kb == nk - 1)),
                                      wait=w, inc=([(te, 1)] if hs == 1 else []))
                            st["probs_free"][pbuf] = te.n
                            if kb == nk - 1:
                                fin_todo.append(["a", (jj, cc_of[it], par, te.n),
                                                 gctr[0] + 2])
                            return
                        return go

                    for kb in range(nk):
                        gi = st["gg"]
                        st["gg"] += 1
                        ab = gi % 2
                        pbuf = gi % 6
                        for hs in range(2):
                            pb = 64 * hs
                            P.add("te", lambda h, ab=ab, hs=hs, kb=kb, pb=pb, jj=jj, cc=cc:
                                  h.matmul(sAB[ab][:, hs * 512:(hs + 1) * 512],
                                           kT[pb:pb + 64, jj * S + kb * 128: jj * S + (kb + 1) * 128],
                                           qT[pb:pb + 64, jj * S + cc * CH: jj * S + (cc + 1) * CH],
                                           start=True, stop=True, tile_position=(pb, 0)),
                                  wait=([(se, st["arena_free"][ab])] if hs == 0 else []),
                                  inc=([(te, 1)] if hs == 1 else []))
                        m_sc = te.n
                        P.add("se", lambda h, ab=ab, pbuf=pbuf: h.activation(
                            probs[pbuf][:, :], sAB[ab][:, :], AF.Exp, scale=0.125),
                            wait=[(te, m_sc), (te, st["probs_free"][pbuf])],
                            inc=[(se, 1)])
                        st["arena_free"][ab] = se.n
                        m_exp = se.n
                        grp = {}
                        if kb >= 4 * cc:
                            dlt = kb - 4 * cc
                            P.add("ge", lambda h, pbuf=pbuf, delta=dlt * 128:
                                  h.affine_select(
                                      probs[pbuf][:, 0:512], probs[pbuf][:, 0:512],
                                      [[1, 512]], ALU.is_ge, 0.0,
                                      base=-delta, channel_multiplier=-1),
                                  wait=[(se, m_exp), (ge, m_maskt)], inc=[(ge, 1)])
                            m_g = ge.n
                            P.add("ve", lambda h, pbuf=pbuf, dlt=dlt: h.tensor_mul(
                                probs[pbuf][:, 512:1024], probs[pbuf][:, 512:1024],
                                maskt[:, dlt * 512:(dlt + 1) * 512]),
                                wait=[(se, m_exp), (ve, ve.n), (ge, m_maskt)],
                                inc=[(ve, 1)])
                            grp[kb] = (pbuf, [(ge, m_g), (ve, ve.n)])
                        else:
                            grp[kb] = (pbuf, [(se, m_exp)])
                        av_todo.append(mk_av(kb, grp))
                        gctr[0] += 1
                        pump(gctr[0])
                    if cc >= 4:
                        emit_ffn_step()
                pump(float("inf"), drain=True)
                assert not fin_todo and not av_todo
                while st["ffn_left"]:
                    emit_ffn_step()
                P.add("ve", lambda h: h.tensor_copy(gT[:, :], g_t[:, :]),
                      wait=[(ve, st["ffn_g"]), (ve, ve.n)], inc=[(ve, 1)])
                m_gT_mark = ve.n

                if debug:
                    for nm_, sb in (("qT", qT), ("kT", kT), ("vv", vv), ("s", s_sb),
                                    ("hT", hT), ("gT", gT)):
                        P.add("sy", lambda h, a=dbg[nm_], b=sb: h.dma_start(a.ap(), b[:, :]),
                              wait=[(ve, ve.n), (se, se.n), (te, te.n)],
                              inc=[(os_, 16)])
                    m_dbg = os_.n
                    P.add("sy", lambda h: h.nop(), wait=[(os_, m_dbg)])

        stAttn.close()

        with ExitStack() as stC:
            EC = stC.enter_context
            hT_comb = EC(nc.sbuf_tensor("hT_comb", [128, 8 * MY], BF16))
            xmt = [EC(nc.sbuf_tensor(f"xmt{i}", [128, D], F32)) for i in range(4)]
            out_sb = [EC(nc.sbuf_tensor(f"out_sb{i}", [128, D], F32)) for i in range(2)]

            gs_h = gs.h

            def mk_readback(jjs):
                tag = "".join(str(j) for j in jjs)
                def emit(h, tag=tag):
                    with h.register("qr" + tag) as qr, h.register("qs" + tag) as qr2:
                        h.reg_load(qr, qoff_sb[0:1, 0:1])
                        h.reg_mul(qr, qr, 256)
                        last = None
                        for jj in jjs:
                            for r2 in range(2):
                                h.reg_mov(qr2, r2 * 128 * MY)
                                h.reg_add(qr2, qr2, qr)
                                rb = r2 * 4 + jj
                                inst = h.dma_start(
                                    bass.AP(hT_comb, rb * MY, [[8 * MY, 128], [1, MY]]),
                                    bass.AP(ag_outs[jj], qr2, [[MY, 128], [1, MY]]))
                                if last is not None:
                                    last.then_inc(gs_h, 16)
                                last = inst
                        return last
                return emit
            mV2, mT2, mS2 = ve.n, te.n, se.n
            P.add("ge", mk_readback([0, 1, 2]),
                  wait=[(ccsj[0], 2), (ccsj[1], 2), (ccsj[2], 2), (cs, 16 * 8),
                        (ve, mV2)],
                  inc=[(gs, 16)])
            gs.n += 5 * 16
            P.add("ge", mk_readback([3]), wait=[(ccsj[3], 2)], inc=[(gs, 16)])
            gs.n += 1 * 16
            m_rb = gs.n
            xm_marks = {}
            for tb in range(4):
                P.add("sy", lambda h, a=xmt[tb], tb=tb: h.dma_start(
                    a[:, :], xm[tb * 128:(tb + 1) * 128, :]),
                    wait=[(ve, mV2), (te, mT2), (se, mS2)],
                    inc=[(xsb[tb % 4], 16)])
                xm_marks[tb] = (xsb[tb % 4], xsb[tb % 4].n)

            with ExitStack() as stPs4:
                E4 = stPs4.enter_context
                ops = [E4(nc.psum_tensor(f"ops{i}", [128, 512], F32)) for i in range(4)]
                obank_free = [(ve, 0)] * 4
                xm_free = [0, 0, 0, 0]
                osb_free = [(osb2[0], 0), (osb2[1], 0)]
                m_gT = ve.n

                for tb in range(16):
                    xb = tb % 4
                    ob = tb % 2
                    if tb >= 4:
                        P.add("sy", lambda h, a=xmt[xb], tb=tb: h.dma_start(
                            a[:, :], xm[tb * 128:(tb + 1) * 128, :]),
                            wait=[(ve, xm_free[xb])], inc=[(xsb[xb], 16)])
                        xm_marks[tb] = (xsb[xb], xsb[xb].n)
                    m_xm = xm_marks[tb]
                    m_nn = [0, 0]
                    for nn in range(2):
                        bk = (2 * tb + nn) % 4
                        for rb in range(8):
                            P.add("te", lambda h, bk=bk, rb=rb, tb=tb, nn=nn: h.matmul(
                                ops[bk][:, :],
                                hT_comb[:, rb * MY + tb * 128: rb * MY + (tb + 1) * 128],
                                wo_bf[:, rb * D + nn * 512: rb * D + (nn + 1) * 512],
                                start=(rb == 0), stop=False),
                                wait=([(gs, m_rb), obank_free[bk], wmarks["wo"]]
                                      if rb == 0 else []))
                        P.add("te", lambda h, bk=bk, tb=tb, nn=nn: h.matmul(
                            ops[bk][:, :],
                            gT[0:64, tb * 128:(tb + 1) * 128],
                            wout_bf[0:64, nn * 512:(nn + 1) * 512],
                            start=False, stop=False),
                            wait=[(ve, m_gT), wmarks["wout"]])
                        P.add("te", lambda h, bk=bk, nn=nn: h.matmul(
                            ops[bk][:, :], ones1[0:1, 0:128],
                            bo_sb[0:1, nn * 512:(nn + 1) * 512],
                            start=False, stop=False))
                        P.add("te", lambda h, bk=bk, nn=nn: h.matmul(
                            ops[bk][:, :], ones1[0:1, 0:128],
                            bout_sb[0:1, nn * 512:(nn + 1) * 512],
                            start=False, stop=True),
                            inc=[(te, 1)])
                        m_mm = te.n
                        P.add("ve", lambda h, bk=bk, ob=ob, xb=xb, nn=nn: h.tensor_tensor(
                            out_sb[ob][:, nn * 512:(nn + 1) * 512],
                            xmt[xb][:, nn * 512:(nn + 1) * 512],
                            ops[bk][:, :], ALU.add),
                            wait=[(te, m_mm), m_xm, osb_free[ob],
                                  (ve, ve.n)],
                            inc=[(ve, 1)])
                        obank_free[bk] = (ve, ve.n)
                        m_nn[nn] = ve.n
                    xm_free[xb] = ve.n
                    P.add("sy", lambda h, ob=ob, tb=tb: h.dma_start(
                        out[tb * 128:(tb + 1) * 128, :], out_sb[ob][:, :]),
                        wait=[(ve, m_nn[1])], inc=[(osb2[ob], 16)])
                    osb_free[ob] = (osb2[ob], osb2[ob].n)
                P.add("sy", lambda h: h.nop(),
                      wait=[(osb2[0], osb2[0].n), (osb2[1], osb2[1].n)])
                P.replay_into(nc)
        stB2.close()

    return nc, dbg



def _shard_inputs(inp):
    f32 = np.float32
    x = np.asarray(inp["L0_attn_x"], f32)
    Wq = np.asarray(inp["L0_attn_Wq"], f32)
    bq = np.asarray(inp["L0_attn_bq"], f32)
    Wk = np.asarray(inp["L0_attn_Wk"], f32)
    bk = np.asarray(inp["L0_attn_bk"], f32)
    Wv = np.asarray(inp["L0_attn_Wv"], f32)
    bv = np.asarray(inp["L0_attn_bv"], f32)
    Wo = np.asarray(inp["L0_attn_Wo"], f32)
    bo = np.asarray(inp["L0_attn_bo"], f32)
    Win = np.asarray(inp["L0_ffn_Win"], f32)
    bin_ = np.asarray(inp["L0_ffn_bin_"], f32)
    W1 = np.asarray(inp["L0_ffn_W1"], f32)
    b1 = np.asarray(inp["L0_ffn_b1"], f32)
    W2 = np.asarray(inp["L0_ffn_W2"], f32)
    b2 = np.asarray(inp["L0_ffn_b2"], f32)
    Wout = np.asarray(inp["L0_ffn_Wout"], f32)
    bout = np.asarray(inp["L0_ffn_bout"], f32)

    C = np.ascontiguousarray
    in_maps = []
    for c in range(8):
        b, r = divmod(c, 2)
        sl = slice(r * 512, (r + 1) * 512)
        in_maps.append({
            "xf": C(x[b]),
            "xm": C(x[b, r * MY:(r + 1) * MY]),
            "wq": C(Wq[:, sl]),
            "wk": C(Wk[:, sl]),
            "wv": C(Wv[:, sl]),
            "wo": C(Wo),
            "bqh": C(bq[sl].reshape(4, 128).T),
            "bkh": C(bk[sl].reshape(4, 128).T),
            "bvh": C(bv[sl].reshape(1, 512)),
            "boh": C(bo.reshape(1, D)),
            "winh": C(Win),
            "binh": C(bin_.reshape(64, 1)),
            "w1h": C(W1),
            "b1h": C(b1),
            "w2h": C(W2),
            "b2h": C(b2.reshape(64, 1)),
            "wouth": C(Wout),
            "bouth": C(bout.reshape(1, D)),
            "qoff": np.array([[r * MY]], dtype=np.int32),
        })
    return in_maps


def kernel(**inputs):
    in_maps = _shard_inputs(inputs)
    nc, _ = build(debug=False)
    res = run_bass_kernel_spmd(nc, in_maps, core_ids=list(range(8)))
    out = np.empty((4, S, D), np.float32)
    for c in range(8):
        b, r = divmod(c, 2)
        out[b, r * MY:(r + 1) * MY] = np.asarray(res.results[c]["out"])
    return out


# version 25 (final)
# speedup vs baseline: 1.0349x; 1.0074x over previous
import numpy as np
import ml_dtypes
from contextlib import ExitStack
import concourse.bass as bass
import concourse.mybir as mybir
from concourse.bass_utils import run_bass_kernel_spmd

F32 = mybir.dt.float32
BF16 = mybir.dt.bfloat16
I32 = mybir.dt.int32
AF = mybir.ActivationFunctionType
ALU = mybir.AluOpType
AX = mybir.AxisListType

D = 1024
S = 4096
MY = 2048
CH = 512
NCH = S // CH
NMYCH = MY // CH
INV_D = 1.0 / D
EPS = 1e-5


class Sem:
    __slots__ = ("h", "n")
    def __init__(self, h):
        self.h = h
        self.n = 0


class Plan:
    ENGS = ("te", "ve", "se", "ge", "sy")

    def __init__(self):
        self.q = {e: [] for e in self.ENGS}

    def add(self, eng, emit, wait=(), inc=()):
        best = {}
        for s, v in wait:
            if v > best.get(id(s), (None, -1))[1]:
                best[id(s)] = (s, v)
        self.q[eng].append((tuple(best.values()), emit, tuple(inc)))
        for s, k in inc:
            s.n += k

    def replay_into(self, nc):
        qs = self.q
        with nc.Block() as blk:
            if qs["te"]:
                blk.tensor(lambda h: Plan._run(h, qs["te"]))
            if qs["ve"]:
                blk.vector(lambda h: Plan._run(h, qs["ve"]))
            if qs["se"]:
                blk.scalar(lambda h: Plan._run(h, qs["se"]))
            if qs["ge"]:
                blk.gpsimd(lambda h: Plan._run(h, qs["ge"]))
            if qs["sy"]:
                blk.sync(lambda h: Plan._run(h, qs["sy"]))
        self.q = {e: [] for e in self.ENGS}

    @staticmethod
    def _run(h, q):
        for wait, emit, inc in q:
            for s, v in wait:
                if v > 0:
                    h.wait_ge(s.h, v)
            inst = emit(h)
            for s, k in inc:
                inst.then_inc(s.h, k)


def build(debug=False):
    nc = bass.Bass("TRN2", target_bir_lowering=False)

    xf = nc.dram_tensor("xf", [S, D], F32, kind="ExternalInput")
    xm = nc.dram_tensor("xm", [MY, D], F32, kind="ExternalInput")
    wq = nc.dram_tensor("wq", [D, 512], F32, kind="ExternalInput")
    wk = nc.dram_tensor("wk", [D, 512], F32, kind="ExternalInput")
    wv = nc.dram_tensor("wv", [D, 512], F32, kind="ExternalInput")
    wo = nc.dram_tensor("wo", [D, D], F32, kind="ExternalInput")
    bqh = nc.dram_tensor("bqh", [128, 4], F32, kind="ExternalInput")
    bkh = nc.dram_tensor("bkh", [128, 4], F32, kind="ExternalInput")
    bvh = nc.dram_tensor("bvh", [1, 512], F32, kind="ExternalInput")
    boh = nc.dram_tensor("boh", [1, D], F32, kind="ExternalInput")
    winh = nc.dram_tensor("winh", [D, 64], F32, kind="ExternalInput")
    binh = nc.dram_tensor("binh", [64, 1], F32, kind="ExternalInput")
    w1h = nc.dram_tensor("w1h", [64, 16], F32, kind="ExternalInput")
    b1h = nc.dram_tensor("b1h", [64, 16], F32, kind="ExternalInput")
    w2h = nc.dram_tensor("w2h", [64, 16], F32, kind="ExternalInput")
    b2h = nc.dram_tensor("b2h", [64, 1], F32, kind="ExternalInput")
    wouth = nc.dram_tensor("wouth", [64, D], F32, kind="ExternalInput")
    bouth = nc.dram_tensor("bouth", [1, D], F32, kind="ExternalInput")
    qoff = nc.dram_tensor("qoff", [1, 1], I32, kind="ExternalInput")
    out = nc.dram_tensor("out", [MY, D], F32, kind="ExternalOutput")
    ag_in2 = nc.dram_tensor("ag_in2", [1024, MY], BF16)
    ag_outs = [nc.dram_tensor(f"ag_out{j}", [512, MY], BF16) for j in range(4)]

    dbg = {}
    if debug:
        dbg["qT"] = nc.dram_tensor("d_qT", [128, 4 * S], BF16, kind="ExternalOutput")
        dbg["kT"] = nc.dram_tensor("d_kT", [128, 4 * S], BF16, kind="ExternalOutput")
        dbg["vv"] = nc.dram_tensor("d_vv", [128, 8 * 32 * 65], BF16, kind="ExternalOutput")
        dbg["s"] = nc.dram_tensor("d_s", [64, MY], F32, kind="ExternalOutput")
        dbg["hT"] = nc.dram_tensor("d_hT", [128, 4 * S], BF16, kind="ExternalOutput")
        dbg["gT"] = nc.dram_tensor("d_gT", [64, MY], BF16, kind="ExternalOutput")

    with ExitStack() as stP:
        EP = stP.enter_context
        stAttn = ExitStack()
        qT = stAttn.enter_context(nc.sbuf_tensor("qT", [128, 4 * S], BF16, side="right"))
        kT = stAttn.enter_context(nc.sbuf_tensor("kT", [128, 4 * S], BF16, side="right"))
        vv = stAttn.enter_context(nc.sbuf_tensor("vv", [128, 8 * 32 * 65], BF16, side="right"))
        s_sb = stAttn.enter_context(nc.sbuf_tensor("s_sb", [64, MY], F32, side="right"))
        gT = EP(nc.sbuf_tensor("gT", [64, MY], BF16))
        ident = EP(nc.sbuf_tensor("ident", [128, 128], BF16))
        ones1 = EP(nc.sbuf_tensor("ones1", [1, 128], BF16))
        bq_sb = EP(nc.sbuf_tensor("bq_sb", [128, 4], F32))
        bk_sb = EP(nc.sbuf_tensor("bk_sb", [128, 4], F32))
        bv_sb = EP(nc.sbuf_tensor("bv_sb", [1, 512], BF16))
        bo_sb = EP(nc.sbuf_tensor("bo_sb", [1, D], BF16))
        bout_sb = EP(nc.sbuf_tensor("bout_sb", [1, D], BF16))
        bin_sb = EP(nc.sbuf_tensor("bin_sb", [64, 1], F32))
        w1_sb = EP(nc.sbuf_tensor("w1_sb", [64, 16], F32))
        b1_sb = EP(nc.sbuf_tensor("b1_sb", [64, 16], F32))
        w2_sb = EP(nc.sbuf_tensor("w2_sb", [64, 16], F32))
        b2_sb = EP(nc.sbuf_tensor("b2_sb", [64, 1], F32))
        qoff_sb = EP(nc.sbuf_tensor("qoff_sb", [1, 1], I32))
        eps_sb = EP(nc.sbuf_tensor("eps_sb", [128, 1], F32))

        te = Sem(EP(nc.semaphore("te_s")))
        ve = Sem(EP(nc.semaphore("ve_s")))
        se = Sem(EP(nc.semaphore("se_s")))
        ge = Sem(EP(nc.semaphore("ge_s")))
        cs = Sem(EP(nc.semaphore("cs_s")))
        xsb = [Sem(EP(nc.semaphore(f"xsb{i}_s"))) for i in range(4)]
        wsg = {k: Sem(EP(nc.semaphore(f"ws_{k}_s")))
               for k in ("wc", "wq", "wk", "wv", "win", "wo", "wout")}
        gs = Sem(EP(nc.semaphore("gs_s")))
        agst = [Sem(EP(nc.semaphore(f"agst{i}_s"))) for i in range(8)]
        ccsj = [Sem(EP(nc.semaphore(f"ccsj{i}_s"))) for i in range(4)]
        osb2 = [Sem(EP(nc.semaphore(f"osb2{i}_s"))) for i in range(2)]
        os_ = Sem(EP(nc.semaphore("os_s")))

        P = Plan()

        for src, dst in ((bqh, bq_sb), (bkh, bk_sb), (binh, bin_sb),
                         (w1h, w1_sb), (b1h, b1_sb), (w2h, w2_sb), (b2h, b2_sb),
                         (qoff, qoff_sb)):
            P.add("sy", lambda h, a=dst, b=src: h.dma_start(a[:, :], b.ap()),
                  inc=[(cs, 16)])
        for src, dst in ((bvh, bv_sb), (boh, bo_sb), (bouth, bout_sb)):
            P.add("ge", lambda h, a=dst, b=src: h.dma_start(a[:, :], b.ap()),
                  inc=[(wsg["wc"], 16)])
        m_bv = wsg["wc"].n
        P.replay_into(nc)

        with ExitStack() as stA:
            EA = stA.enter_context
            wq_bf = EA(nc.sbuf_tensor("wq_bf", [128, 8 * 512], BF16, side="right"))
            wk_bf = EA(nc.sbuf_tensor("wk_bf", [128, 8 * 512], BF16, side="right"))
            wv_bf = EA(nc.sbuf_tensor("wv_bf", [128, 8 * 512], BF16, side="right"))
            win_bf = EA(nc.sbuf_tensor("win_bf", [128, 8 * 64], BF16, side="right"))
            x_t = [EA(nc.sbuf_tensor(f"x_t{i}", [128, D], F32, side="right")) for i in range(4)]
            xn_t = [EA(nc.sbuf_tensor(f"xn_t{i}", [128, D], BF16, side="right")) for i in range(4)]
            xnT = [EA(nc.sbuf_tensor(f"xnT{i}", [128, 8 * CH], BF16, side="right")) for i in range(2)]
            sx = EA(nc.sbuf_tensor("sx", [128, 8], F32, side="right"))
            ssq = EA(nc.sbuf_tensor("ssq", [128, 8], F32, side="right"))
            nm = EA(nc.sbuf_tensor("nm", [128, 8], F32, side="right"))
            var = EA(nc.sbuf_tensor("var", [128, 8], F32, side="right"))
            m2t = EA(nc.sbuf_tensor("m2t", [128, 8], F32, side="right"))
            tln = EA(nc.sbuf_tensor("tln", [128, 8], F32, side="right"))
            rstd = EA(nc.sbuf_tensor("rstd", [128, 8], F32, side="right"))

            with ExitStack() as stPs1:
                E1 = stPs1.enter_context
                ps_tr = [E1(nc.psum_tensor(f"ps_tr{i}", [128, 1024], BF16)) for i in range(2)]
                ps_p = [E1(nc.psum_tensor(f"ps_p{i}", [128, 512], F32)) for i in range(4)]

                P.add("ge", lambda h: h.iota(ident[:, :], [[1, 128]], base=0,
                                             channel_multiplier=-1,
                                             allow_small_or_imprecise_dtypes=True),
                      inc=[(ge, 1)])
                m_iota = ge.n
                P.add("ve", lambda h: h.tensor_single_scalar(
                    ident[:, :], ident[:, :], 0.0, ALU.is_equal),
                    wait=[(ge, m_iota)], inc=[(ve, 1)])
                P.add("ve", lambda h: h.memset(ones1[:, :], 1.0),
                      wait=[(ve, ve.n)], inc=[(ve, 1)])
                P.add("ve", lambda h: h.memset(
                    bass.AP(vv, 64, [[8 * 32 * 65, 128], [65, 8 * 32], [1, 1]]), 1.0),
                    wait=[(ve, ve.n)], inc=[(ve, 1)])
                P.add("ve", lambda h: h.memset(eps_sb[:, :], EPS),
                      wait=[(ve, ve.n)], inc=[(ve, 1)])
                m_const = ve.n

                wmarks = {}
                for name, wsrc, wdst, nblk, bw in (
                        ("wq", wq, wq_bf, 8, 512), ("wk", wk, wk_bf, 8, 512),
                        ("wv", wv, wv_bf, 8, 512), ("win", winh, win_bf, 8, 64)):
                    for db in range(nblk):
                        P.add("ge", lambda h, a=wdst, b=wsrc, db=db, bw=bw, name=name:
                              h.dma_start(a[:, db * bw:(db + 1) * bw],
                                          b[db * 128:(db + 1) * 128, :]),
                              inc=[(wsg[name], 16)])
                    wmarks[name] = (wsg[name], wsg[name].n)

                xt_free = [0, 0, 0, 0]
                xn_free = [0, 0, 0, 0]
                tr_free = [0, 0]
                bank_free = [(ve, 0)] * 4
                ch_done = {}
                bank_i = 0

                for c in range(NCH + NMYCH):
                    bc = c % 2
                    sc4 = (c % 2) * 4
                    full = c < NCH
                    for half in range(2):
                        pair = (2 * half, 2 * half + 1)
                        last_sum = 0
                        for ti in pair:
                            xb = ti
                            col = sc4 + ti
                            if full:
                                src = xf[c * CH + ti * 128: c * CH + (ti + 1) * 128, :]
                            else:
                                base = (c - NCH) * CH + ti * 128
                                src = xm[base: base + 128, :]
                            P.add("sy", lambda h, a=x_t[xb], b=src: h.dma_start(a[:, :], b),
                                  wait=[(ve, xt_free[xb])], inc=[(xsb[xb], 16)])
                            m_x = (xsb[xb], xsb[xb].n)
                            P.add("se", lambda h, a=x_t[xb], o=xn_t[ti], col=col:
                                h.activation(
                                    o[:, :], a[:, :], AF.Identity,
                                    accum_out=sx[:, col:col + 1]),
                                wait=[m_x, (se, se.n), (te, xn_free[ti])],
                                inc=[(se, 1)])
                            last_sum = se.n
                            P.add("se", lambda h, a=x_t[xb], o=xn_t[ti], col=col:
                                h.activation(
                                    o[:, :], a[:, :], AF.Square,
                                    accum_out=ssq[:, col:col + 1]),
                                wait=[m_x, (se, se.n), (te, xn_free[ti])],
                                inc=[(se, 1)])
                            last_sq = se.n
                        s2 = sc4 + 2 * half
                        P.add("ve", lambda h, s2=s2: h.tensor_scalar(
                            nm[:, s2:s2 + 2], sx[:, s2:s2 + 2], -INV_D, None, ALU.mult),
                            wait=[(se, last_sum), (ve, ve.n)], inc=[(ve, 1)])
                        P.add("ve", lambda h, s2=s2: h.tensor_mul(
                            m2t[:, s2:s2 + 2], nm[:, s2:s2 + 2], nm[:, s2:s2 + 2]),
                            wait=[(ve, ve.n)], inc=[(ve, 1)])
                        P.add("ve", lambda h, s2=s2: h.scalar_tensor_tensor(
                            var[:, s2:s2 + 2], ssq[:, s2:s2 + 2], INV_D, m2t[:, s2:s2 + 2],
                            ALU.mult, ALU.subtract),
                            wait=[(ve, ve.n), (se, last_sq)], inc=[(ve, 1)])
                        m_var = ve.n
                        P.add("se", lambda h, s2=s2: h.activation(
                            tln[:, s2:s2 + 2], var[:, s2:s2 + 2], AF.Ln,
                            bias=eps_sb[:, 0:1]),
                            wait=[(ve, m_var), (se, se.n), (ve, m_const)], inc=[(se, 1)])
                        P.add("se", lambda h, s2=s2: h.activation(
                            rstd[:, s2:s2 + 2], tln[:, s2:s2 + 2], AF.Exp, scale=-0.5),
                            wait=[(se, se.n)], inc=[(se, 1)])
                        m_rstd = se.n
                        m_trs = {}
                        for ti in pair:
                            xb = ti
                            col = sc4 + ti
                            P.add("ve", lambda h, a=xn_t[ti], b=x_t[xb], col=col: h.tensor_scalar(
                                a[:, :], b[:, :], nm[:, col:col + 1], rstd[:, col:col + 1],
                                ALU.add, ALU.mult),
                                wait=[(se, m_rstd), (te, xn_free[ti]), (ve, ve.n)],
                                inc=[(ve, 1)])
                            xt_free[xb] = ve.n
                            m_xn = ve.n
                            tb = ti % 2
                            for db in range(8):
                                P.add("te", lambda h, a=ps_tr[tb], b=xn_t[ti], db=db: h.transpose(
                                    a[:, db * 128:(db + 1) * 128],
                                    b[:, db * 128:(db + 1) * 128], ident[:, :]),
                                    wait=([(ve, m_xn), (ve, tr_free[tb]), (ve, m_const)]
                                          if db == 0 else []),
                                    inc=([(te, 1)] if db == 7 else []))
                            xn_free[ti] = te.n
                            m_trs[ti] = te.n
                        for ti in pair:
                            tb = ti % 2
                            P.add("ve", lambda h, a=xnT[bc], b=ps_tr[tb], ti=ti: h.tensor_copy(
                                bass.AP(a, ti * 128, [[8 * CH, 128], [CH, 8], [1, 128]]),
                                b[:, :]),
                                wait=[(te, m_trs[ti]), (te, ch_done.get(c - 2, 0)), (ve, ve.n)],
                                inc=[(ve, 1)])
                            tr_free[tb] = ve.n
                    m_evac = ve.n
                    if full:
                        for wb, bias_t, dst, wmk in ((wq_bf, bq_sb, qT, wmarks["wq"]),
                                                     (wk_bf, bk_sb, kT, wmarks["wk"])):
                            for j in range(4):
                                bk = bank_i % 4
                                bank_i += 1
                                for db in range(8):
                                    P.add("te", lambda h, bk=bk, wb=wb, db=db, j=j, bc=bc:
                                          h.matmul(ps_p[bk][:, :],
                                                   wb[:, db * 512 + j * 128: db * 512 + (j + 1) * 128],
                                                   xnT[bc][:, db * CH:(db + 1) * CH],
                                                   start=(db == 0), stop=(db == 7)),
                                          wait=([(ve, m_evac), bank_free[bk], wmk]
                                                if db == 0 else []),
                                          inc=([(te, 1)] if db == 7 else []))
                                m_mm = te.n
                                P.add("se", lambda h, bk=bk, dst=dst, bias_t=bias_t, j=j, c=c:
                                      h.activation(dst[:, j * S + c * CH: j * S + (c + 1) * CH],
                                                   ps_p[bk][:, :], AF.Identity,
                                                   bias=bias_t[:, j:j + 1]),
                                      wait=[(se, se.n), (te, m_mm), (cs, 16 * 8)],
                                      inc=[(se, 1)])
                                bank_free[bk] = (se, se.n)
                        for ti in range(4):
                            bk = bank_i % 4
                            bank_i += 1
                            for db in range(8):
                                P.add("te", lambda h, bk=bk, db=db, ti=ti, bc=bc:
                                      h.matmul(ps_p[bk][:, :],
                                               xnT[bc][:, db * CH + ti * 128: db * CH + (ti + 1) * 128],
                                               wv_bf[:, db * 512:(db + 1) * 512],
                                               start=(db == 0), stop=False),
                                      wait=([(ve, m_evac), bank_free[bk],
                                             wmarks["wv"]] if db == 0 else []))
                            P.add("te", lambda h, bk=bk: h.matmul(
                                ps_p[bk][:, :], ones1[0:1, 0:128], bv_sb[0:1, :],
                                start=False, stop=True),
                                wait=[(wsg["wc"], m_bv)], inc=[(te, 1)])
                            m_mm = te.n
                            tb_idx = c * 4 + ti
                            P.add("ve", lambda h, bk=bk, tb_idx=tb_idx: h.tensor_copy(
                                bass.AP(vv, tb_idx * 65,
                                        [[8 * 32 * 65, 128], [32 * 65, 8], [1, 64]]),
                                ps_p[bk][:, :]),
                                wait=[(te, m_mm), (ve, ve.n)], inc=[(ve, 1)])
                            bank_free[bk] = (ve, ve.n)
                    else:
                        bk = bank_i % 4
                        bank_i += 1
                        for db in range(8):
                            P.add("te", lambda h, bk=bk, db=db, bc=bc: h.matmul(
                                ps_p[bk][0:64, :],
                                win_bf[:, db * 64:(db + 1) * 64],
                                xnT[bc][:, db * CH:(db + 1) * CH],
                                start=(db == 0), stop=(db == 7)),
                                wait=([(ve, m_evac), bank_free[bk], wmarks["win"]]
                                      if db == 0 else []),
                                inc=([(te, 1)] if db == 7 else []))
                        m_mm = te.n
                        cc = c - NCH
                        P.add("se", lambda h, bk=bk, cc=cc: h.activation(
                            s_sb[0:64, cc * CH:(cc + 1) * CH], ps_p[bk][0:64, :],
                            AF.Identity, bias=bin_sb[0:64, 0:1]),
                            wait=[(se, se.n), (te, m_mm)], inc=[(se, 1)])
                        bank_free[bk] = (se, se.n)
                    ch_done[c] = te.n

                P.replay_into(nc)

        stB2 = ExitStack()
        hT = stB2.enter_context(nc.sbuf_tensor("hT", [128, 4 * S], BF16))
        wo_bf = stB2.enter_context(nc.sbuf_tensor("wo_bf", [128, 8 * D], BF16))
        wout_bf = stB2.enter_context(nc.sbuf_tensor("wout_bf", [64, D], BF16))
        for db in range(8):
            P.add("ge", lambda h, db=db: h.dma_start(
                wo_bf[:, db * D:(db + 1) * D], wo[db * 128:(db + 1) * 128, :]),
                inc=[(wsg["wo"], 16)])
        wmarks["wo"] = (wsg["wo"], wsg["wo"].n)
        P.add("ge", lambda h: h.dma_start(wout_bf[:, :], wouth.ap()),
              inc=[(wsg["wout"], 16)])
        wmarks["wout"] = (wsg["wout"], wsg["wout"].n)
        with ExitStack() as stB:
            EB = stB.enter_context
            probs = [EB(nc.sbuf_tensor(f"probs{i}", [128, 1024], BF16, side="right")) for i in range(7)]
            hTun = EB(nc.sbuf_tensor("hTun", [64, 1024], BF16, side="right"))
            linv_f = EB(nc.sbuf_tensor("linv_f", [1, 1024], F32, side="right"))
            linv_b = EB(nc.sbuf_tensor("linv_b", [1, 1024], BF16, side="right"))
            h_t = EB(nc.sbuf_tensor("h_t", [64, MY], F32, side="right"))
            g_t = EB(nc.sbuf_tensor("g_t", [64, MY], F32, side="right"))
            maskt = EB(nc.sbuf_tensor("maskt", [128, 4 * 512], BF16, side="right"))
            with ExitStack() as stPs2:
                E2 = stPs2.enter_context
                sAB = [E2(nc.psum_tensor(f"sAB{i}", [128, 1024], F32)) for i in range(2)]
                hpsT = [E2(nc.psum_tensor(f"hpsT{i}", [65, 1024], F32)) for i in range(2)]

                P.add("ve", lambda h: h.memset(maskt[:, :], 1.0), wait=[(ve, ve.n)],
                      inc=[(ve, 1)])
                m_ones = ve.n
                for dlt in range(4):
                    P.add("ge", lambda h, dlt=dlt: h.affine_select(
                        maskt[:, dlt * 512:(dlt + 1) * 512],
                        maskt[:, dlt * 512:(dlt + 1) * 512],
                        [[1, 512]], ALU.is_ge, 0.0,
                        base=-dlt * 128, channel_multiplier=-1),
                        wait=[(ve, m_ones), (ge, ge.n)], inc=[(ge, 1)])
                m_maskt = ge.n

                st = {
                    "arena_free": [0, 0],
                    "probs_free": [0] * 7,
                    "hps_free": [0, 0],
                    "linv_rd": 0,
                    "hTun_rd": 0,
                    "gg": 0,
                    "ffn_g": 0,
                    "ffn_left": list(range(16)),
                }

                def emit_ffn_step():
                    if not st["ffn_left"]:
                        return
                    i = st["ffn_left"].pop(0)
                    P.add("ve", lambda h, i=i: h.tensor_scalar(
                        h_t[:, :], s_sb[:, :], w1_sb[:, i:i + 1], b1_sb[:, i:i + 1],
                        ALU.mult, ALU.add),
                        wait=[(ve, st["ffn_g"]), (ve, ve.n)], inc=[(ve, 1)])
                    P.add("ve", lambda h: h.tensor_scalar_max(h_t[:, :], h_t[:, :], 0.0),
                          wait=[(ve, ve.n)], inc=[(ve, 1)])
                    m_h = ve.n
                    if i == 0:
                        P.add("ve", lambda h: h.tensor_scalar(
                            g_t[:, :], h_t[:, :], w2_sb[:, 0:1], b2_sb[:, 0:1],
                            ALU.mult, ALU.add),
                            wait=[(ve, m_h)], inc=[(ve, 1)])
                    else:
                        P.add("ve", lambda h, i=i: h.scalar_tensor_tensor(
                            g_t[:, :], h_t[:, :], w2_sb[:, i:i + 1], g_t[:, :],
                            ALU.mult, ALU.add),
                            wait=[(ve, m_h)], inc=[(ve, 1)])
                    st["ffn_g"] = ve.n

                def emit_fin_a(jj, cc, par, m_av):
                    P.add("se", lambda h, par=par: h.activation(
                        linv_f[0:1, :], hpsT[par][64:65, :], AF.Ln),
                        wait=[(te, m_av), (se, se.n), (te, st["linv_rd"])],
                        inc=[(se, 1)])
                    P.add("se", lambda h: h.activation(
                        linv_b[0:1, :], linv_f[0:1, :], AF.Exp, scale=-1.0),
                        wait=[(se, se.n)], inc=[(se, 1)])
                    m_linv = se.n
                    P.add("ve", lambda h, par=par: h.tensor_copy(
                        hTun[:, :], hpsT[par][0:64, :]),
                        wait=[(te, m_av), (ve, ve.n), (ve, st["hTun_rd"])],
                        inc=[(ve, 1)])
                    m_cp = ve.n
                    return (jj, cc, par, m_linv, m_cp)

                def emit_fin_b(fin):
                    jj, cc, par, m_linv, m_cp = fin
                    for hs in range(2):
                        P.add("te", lambda h, par=par, hs=hs: h.matmul(
                            hpsT[par][0:64, hs * 512:(hs + 1) * 512],
                            ones1[0:1, 0:64], linv_b[0:1, hs * 512:(hs + 1) * 512],
                            start=True, stop=True),
                            wait=([(se, m_linv), (ve, m_cp)] if hs == 0 else []),
                            inc=([(te, 1)] if hs == 1 else []))
                    m_rep = te.n
                    st["linv_rd"] = te.n
                    for hs in range(2):
                        P.add("ve", lambda h, jj=jj, cc=cc, par=par, hs=hs: h.tensor_tensor(
                            hT[hs * 64:hs * 64 + 64, jj * S + cc * CH: jj * S + (cc + 1) * CH],
                            hTun[:, hs * 512:(hs + 1) * 512],
                            hpsT[par][0:64, hs * 512:(hs + 1) * 512], ALU.mult),
                            wait=[(te, m_rep), (ve, ve.n)], inc=[(ve, 1)])
                    st["hTun_rd"] = ve.n
                    st["hps_free"][par] = ve.n
                    return ve.n

                pair_done = {}
                gctr = [0]
                cc_of = {it: divmod(it, 8)[1] for it in range(32)}

                def emit_ag(jj, half):
                    P.add("sy", lambda h, jj=jj, half=half: h.dma_start(
                        bass.AP(ag_in2, (jj * 2 + half) * 128 * MY, [[MY, 128], [1, MY]]),
                        bass.AP(hT, jj * S + half * MY, [[4 * S, 128], [1, MY]])),
                        wait=[(ve, pair_done[(jj, half)])],
                        inc=[(agst[jj * 2 + half], 16)])
                    P.add("ge", lambda h, jj=jj, half=half: h.collective_compute(
                        "AllGather", ALU.bypass,
                        replica_groups=[[0, 1], [2, 3], [4, 5], [6, 7]],
                        ins=[ag_in2[(jj * 2 + half) * 128:(jj * 2 + half + 1) * 128, :].opt()],
                        outs=[ag_outs[jj][half * 256:(half + 1) * 256, :].opt()]),
                        wait=[(agst[jj * 2 + half], 16), (ge, ge.n)],
                        inc=[(ccsj[jj], 1)])

                av_todo = []
                fin_todo = []
                AV_LAG = 3

                def emit_one_av():
                    if av_todo:
                        av_todo.pop(0)()

                def pump(gnow, drain=False):
                    changed = True
                    while changed:
                        changed = False
                        if fin_todo and fin_todo[0][2] <= gnow:
                            stage, payload, _ = fin_todo.pop(0)
                            if stage == "a":
                                pb_ = emit_fin_a(*payload)
                                fin_todo.insert(0, ["b", pb_, gnow + 2])
                            else:
                                jprev, ccprev = payload[0], payload[1]
                                mkv = emit_fin_b(payload)
                                if ccprev == 3:
                                    pair_done[(jprev, 0)] = mkv
                                    emit_ag(jprev, 0)
                                elif ccprev == 7:
                                    pair_done[(jprev, 1)] = mkv
                                    emit_ag(jprev, 1)
                            changed = True
                        if len(av_todo) > (0 if drain else AV_LAG):
                            emit_one_av()
                            changed = True

                for it in range(32):
                    jj, cc = divmod(it, 8)
                    nk = 4 * cc + 4
                    par = it % 2

                    def mk_av(kb, grp, jj=jj, nk=nk, par=par, it=it):
                        def go(kb=kb, grp=grp, jj=jj, nk=nk, par=par, it=it):
                            pbuf, ready = grp[kb]
                            for hs in range(2):
                                w = list(ready) if hs == 0 else []
                                if kb == 0 and hs == 0:
                                    w = w + [(ve, st["hps_free"][par])]
                                hh = 2 * jj + hs
                                P.add("te", lambda h, par=par, hh=hh, kb=kb, pbuf=pbuf,
                                      hs=hs, nk=nk:
                                      h.matmul(hpsT[par][0:65, hs * 512:(hs + 1) * 512],
                                               vv[:, (hh * 32 + kb) * 65:(hh * 32 + kb) * 65 + 65],
                                               probs[pbuf][:, hs * 512:(hs + 1) * 512],
                                               start=(kb == 0), stop=(kb == nk - 1)),
                                      wait=w, inc=([(te, 1)] if hs == 1 else []))
                            st["probs_free"][pbuf] = te.n
                            if kb == nk - 1:
                                fin_todo.append(["a", (jj, cc_of[it], par, te.n),
                                                 gctr[0] + 2])
                            return
                        return go

                    for kb in range(nk):
                        gi = st["gg"]
                        st["gg"] += 1
                        ab = gi % 2
                        pbuf = gi % 7
                        for hs in range(2):
                            pb = 64 * hs
                            P.add("te", lambda h, ab=ab, hs=hs, kb=kb, pb=pb, jj=jj, cc=cc:
                                  h.matmul(sAB[ab][:, hs * 512:(hs + 1) * 512],
                                           kT[pb:pb + 64, jj * S + kb * 128: jj * S + (kb + 1) * 128],
                                           qT[pb:pb + 64, jj * S + cc * CH: jj * S + (cc + 1) * CH],
                                           start=True, stop=True, tile_position=(pb, 0)),
                                  wait=([(se, st["arena_free"][ab])] if hs == 0 else []),
                                  inc=([(te, 1)] if hs == 1 else []))
                        m_sc = te.n
                        P.add("se", lambda h, ab=ab, pbuf=pbuf: h.activation(
                            probs[pbuf][:, :], sAB[ab][:, :], AF.Exp, scale=0.125),
                            wait=[(te, m_sc), (te, st["probs_free"][pbuf])],
                            inc=[(se, 1)])
                        st["arena_free"][ab] = se.n
                        m_exp = se.n
                        grp = {}
                        if kb >= 4 * cc:
                            dlt = kb - 4 * cc
                            P.add("ge", lambda h, pbuf=pbuf, delta=dlt * 128:
                                  h.affine_select(
                                      probs[pbuf][:, 0:512], probs[pbuf][:, 0:512],
                                      [[1, 512]], ALU.is_ge, 0.0,
                                      base=-delta, channel_multiplier=-1),
                                  wait=[(se, m_exp), (ge, m_maskt)], inc=[(ge, 1)])
                            m_g = ge.n
                            P.add("ve", lambda h, pbuf=pbuf, dlt=dlt: h.tensor_mul(
                                probs[pbuf][:, 512:1024], probs[pbuf][:, 512:1024],
                                maskt[:, dlt * 512:(dlt + 1) * 512]),
                                wait=[(se, m_exp), (ve, ve.n), (ge, m_maskt)],
                                inc=[(ve, 1)])
                            grp[kb] = (pbuf, [(ge, m_g), (ve, ve.n)])
                        else:
                            grp[kb] = (pbuf, [(se, m_exp)])
                        av_todo.append(mk_av(kb, grp))
                        gctr[0] += 1
                        pump(gctr[0])
                    if cc >= 4:
                        emit_ffn_step()
                pump(float("inf"), drain=True)
                assert not fin_todo and not av_todo
                while st["ffn_left"]:
                    emit_ffn_step()
                P.add("ve", lambda h: h.tensor_copy(gT[:, :], g_t[:, :]),
                      wait=[(ve, st["ffn_g"]), (ve, ve.n)], inc=[(ve, 1)])
                m_gT_mark = ve.n

                if debug:
                    for nm_, sb in (("qT", qT), ("kT", kT), ("vv", vv), ("s", s_sb),
                                    ("hT", hT), ("gT", gT)):
                        P.add("sy", lambda h, a=dbg[nm_], b=sb: h.dma_start(a.ap(), b[:, :]),
                              wait=[(ve, ve.n), (se, se.n), (te, te.n)],
                              inc=[(os_, 16)])
                    m_dbg = os_.n
                    P.add("sy", lambda h: h.nop(), wait=[(os_, m_dbg)])

        stAttn.close()

        with ExitStack() as stC:
            EC = stC.enter_context
            hT_comb = EC(nc.sbuf_tensor("hT_comb", [128, 8 * MY], BF16))
            xmt = [EC(nc.sbuf_tensor(f"xmt{i}", [128, D], F32)) for i in range(4)]
            out_sb = [EC(nc.sbuf_tensor(f"out_sb{i}", [128, D], F32)) for i in range(2)]

            gs_h = gs.h

            def mk_readback(jjs):
                tag = "".join(str(j) for j in jjs)
                def emit(h, tag=tag):
                    with h.register("qr" + tag) as qr, h.register("qs" + tag) as qr2:
                        h.reg_load(qr, qoff_sb[0:1, 0:1])
                        h.reg_mul(qr, qr, 256)
                        last = None
                        for jj in jjs:
                            for r2 in range(2):
                                h.reg_mov(qr2, r2 * 128 * MY)
                                h.reg_add(qr2, qr2, qr)
                                rb = r2 * 4 + jj
                                inst = h.dma_start(
                                    bass.AP(hT_comb, rb * MY, [[8 * MY, 128], [1, MY]]),
                                    bass.AP(ag_outs[jj], qr2, [[MY, 128], [1, MY]]))
                                if last is not None:
                                    last.then_inc(gs_h, 16)
                                last = inst
                        return last
                return emit
            mV2, mT2, mS2 = ve.n, te.n, se.n
            P.add("ge", mk_readback([0, 1, 2]),
                  wait=[(ccsj[0], 2), (ccsj[1], 2), (ccsj[2], 2), (cs, 16 * 8),
                        (ve, mV2)],
                  inc=[(gs, 16)])
            gs.n += 5 * 16
            P.add("ge", mk_readback([3]), wait=[(ccsj[3], 2)], inc=[(gs, 16)])
            gs.n += 1 * 16
            m_rb = gs.n
            xm_marks = {}
            for tb in range(4):
                P.add("sy", lambda h, a=xmt[tb], tb=tb: h.dma_start(
                    a[:, :], xm[tb * 128:(tb + 1) * 128, :]),
                    wait=[(ve, mV2), (te, mT2), (se, mS2)],
                    inc=[(xsb[tb % 4], 16)])
                xm_marks[tb] = (xsb[tb % 4], xsb[tb % 4].n)

            with ExitStack() as stPs4:
                E4 = stPs4.enter_context
                ops = [E4(nc.psum_tensor(f"ops{i}", [128, 512], F32)) for i in range(4)]
                obank_free = [(ve, 0)] * 4
                xm_free = [0, 0, 0, 0]
                osb_free = [(osb2[0], 0), (osb2[1], 0)]
                m_gT = ve.n

                for tb in range(16):
                    xb = tb % 4
                    ob = tb % 2
                    if tb >= 4:
                        P.add("sy", lambda h, a=xmt[xb], tb=tb: h.dma_start(
                            a[:, :], xm[tb * 128:(tb + 1) * 128, :]),
                            wait=[(ve, xm_free[xb])], inc=[(xsb[xb], 16)])
                        xm_marks[tb] = (xsb[xb], xsb[xb].n)
                    m_xm = xm_marks[tb]
                    m_nn = [0, 0]
                    for nn in range(2):
                        bk = (2 * tb + nn) % 4
                        for rb in range(8):
                            P.add("te", lambda h, bk=bk, rb=rb, tb=tb, nn=nn: h.matmul(
                                ops[bk][:, :],
                                hT_comb[:, rb * MY + tb * 128: rb * MY + (tb + 1) * 128],
                                wo_bf[:, rb * D + nn * 512: rb * D + (nn + 1) * 512],
                                start=(rb == 0), stop=False),
                                wait=([(gs, m_rb), obank_free[bk], wmarks["wo"]]
                                      if rb == 0 else []))
                        P.add("te", lambda h, bk=bk, tb=tb, nn=nn: h.matmul(
                            ops[bk][:, :],
                            gT[0:64, tb * 128:(tb + 1) * 128],
                            wout_bf[0:64, nn * 512:(nn + 1) * 512],
                            start=False, stop=False),
                            wait=[(ve, m_gT), wmarks["wout"]])
                        P.add("te", lambda h, bk=bk, nn=nn: h.matmul(
                            ops[bk][:, :], ones1[0:1, 0:128],
                            bo_sb[0:1, nn * 512:(nn + 1) * 512],
                            start=False, stop=False))
                        P.add("te", lambda h, bk=bk, nn=nn: h.matmul(
                            ops[bk][:, :], ones1[0:1, 0:128],
                            bout_sb[0:1, nn * 512:(nn + 1) * 512],
                            start=False, stop=True),
                            inc=[(te, 1)])
                        m_mm = te.n
                        P.add("ve", lambda h, bk=bk, ob=ob, xb=xb, nn=nn: h.tensor_tensor(
                            out_sb[ob][:, nn * 512:(nn + 1) * 512],
                            xmt[xb][:, nn * 512:(nn + 1) * 512],
                            ops[bk][:, :], ALU.add),
                            wait=[(te, m_mm), m_xm, osb_free[ob],
                                  (ve, ve.n)],
                            inc=[(ve, 1)])
                        obank_free[bk] = (ve, ve.n)
                        m_nn[nn] = ve.n
                    xm_free[xb] = ve.n
                    P.add("sy", lambda h, ob=ob, tb=tb: h.dma_start(
                        out[tb * 128:(tb + 1) * 128, :], out_sb[ob][:, :]),
                        wait=[(ve, m_nn[1])], inc=[(osb2[ob], 16)])
                    osb_free[ob] = (osb2[ob], osb2[ob].n)
                P.add("sy", lambda h: h.nop(),
                      wait=[(osb2[0], osb2[0].n), (osb2[1], osb2[1].n)])
                P.replay_into(nc)
        stB2.close()

    return nc, dbg



def _shard_inputs(inp):
    f32 = np.float32
    x = np.asarray(inp["L0_attn_x"], f32)
    Wq = np.asarray(inp["L0_attn_Wq"], f32)
    bq = np.asarray(inp["L0_attn_bq"], f32)
    Wk = np.asarray(inp["L0_attn_Wk"], f32)
    bk = np.asarray(inp["L0_attn_bk"], f32)
    Wv = np.asarray(inp["L0_attn_Wv"], f32)
    bv = np.asarray(inp["L0_attn_bv"], f32)
    Wo = np.asarray(inp["L0_attn_Wo"], f32)
    bo = np.asarray(inp["L0_attn_bo"], f32)
    Win = np.asarray(inp["L0_ffn_Win"], f32)
    bin_ = np.asarray(inp["L0_ffn_bin_"], f32)
    W1 = np.asarray(inp["L0_ffn_W1"], f32)
    b1 = np.asarray(inp["L0_ffn_b1"], f32)
    W2 = np.asarray(inp["L0_ffn_W2"], f32)
    b2 = np.asarray(inp["L0_ffn_b2"], f32)
    Wout = np.asarray(inp["L0_ffn_Wout"], f32)
    bout = np.asarray(inp["L0_ffn_bout"], f32)

    C = np.ascontiguousarray
    in_maps = []
    for c in range(8):
        b, r = divmod(c, 2)
        sl = slice(r * 512, (r + 1) * 512)
        in_maps.append({
            "xf": C(x[b]),
            "xm": C(x[b, r * MY:(r + 1) * MY]),
            "wq": C(Wq[:, sl]),
            "wk": C(Wk[:, sl]),
            "wv": C(Wv[:, sl]),
            "wo": C(Wo),
            "bqh": C(bq[sl].reshape(4, 128).T),
            "bkh": C(bk[sl].reshape(4, 128).T),
            "bvh": C(bv[sl].reshape(1, 512)),
            "boh": C(bo.reshape(1, D)),
            "winh": C(Win),
            "binh": C(bin_.reshape(64, 1)),
            "w1h": C(W1),
            "b1h": C(b1),
            "w2h": C(W2),
            "b2h": C(b2.reshape(64, 1)),
            "wouth": C(Wout),
            "bouth": C(bout.reshape(1, D)),
            "qoff": np.array([[r * MY]], dtype=np.int32),
        })
    return in_maps


def kernel(**inputs):
    in_maps = _shard_inputs(inputs)
    nc, _ = build(debug=False)
    res = run_bass_kernel_spmd(nc, in_maps, core_ids=list(range(8)))
    out = np.empty((4, S, D), np.float32)
    for c in range(8):
        b, r = divmod(c, 2)
        out[b, r * MY:(r + 1) * MY] = np.asarray(res.results[c]["out"])
    return out
